# version 34
# baseline (speedup 1.0000x reference)
import numpy as np
import ml_dtypes
from contextlib import ExitStack

import concourse.bass as bass
import concourse.mybir as mybir
from concourse.bass_utils import run_bass_kernel_spmd

F32 = mybir.dt.float32
BF16 = mybir.dt.bfloat16
AF = mybir.ActivationFunctionType
ALU = mybir.AluOpType
AX = mybir.AxisListType

BPC = 2
S = 2048
D = 1024
H = 16
E = 8
NT = S // 128
NCH = D // 128
NG = 4
QG = 512
NPAIR = 8
CSH = 20.0
EPS = 1e-5

ENGS = ("tensor", "vector", "scalar", "gpsimd", "sync")


class Op:
    __slots__ = ("eng", "fn", "deps", "sem", "inc_n", "tok", "needed", "tag", "is_dma")

    def __init__(self, eng, fn, sem, inc_n, tag, is_dma):
        self.eng = eng
        self.fn = fn
        self.deps = []
        self.sem = sem
        self.inc_n = inc_n
        self.tok = None
        self.needed = False
        self.tag = tag
        self.is_dma = is_dma


NLANES = 8


class Prog:
    def __init__(self):
        self.ops = {e: [] for e in ENGS}
        self.all = []
        self.slots = {}
        self.eng_sem = {"tensor": "mP", "vector": "mV", "scalar": "mA",
                        "gpsimd": "mG", "sync": "mS"}
        self.qcnt = {"qS": 0, "qG": 0}
        self.lane_last = {}

    def _add_dep(self, o, d, raw):
        if d is None or d is o:
            return
        same = d.eng == o.eng and not (d.is_dma or o.is_dma)
        if same and (not raw or o.eng == "tensor"):
            return
        o.deps.append(d)

    def op(self, eng, fn, deps=(), reads=(), writes=(), dma_sem=None, tag=""):
        is_dma = dma_sem is not None
        if is_dma:
            n = self.qcnt[dma_sem]
            self.qcnt[dma_sem] = n + 1
            lane = n % NLANES
            sem = f"{dma_sem}{lane}"
            o = Op(eng, fn, sem, 16, tag, True)
            o.tok = (sem, 16 * (n // NLANES + 1))
            prev = self.lane_last.get(sem)
            if prev is not None:
                o.deps.append(prev)
            self.lane_last[sem] = o
        else:
            o = Op(eng, fn, self.eng_sem[eng], 1, tag, False)
        for d in deps:
            self._add_dep(o, d, raw=True)
        for r in reads:
            s = self.slots.get(r)
            if s and s["w"] is not None:
                self._add_dep(o, s["w"], raw=True)
        for w in writes:
            s = self.slots.setdefault(w, {"w": None, "r": []})
            self._add_dep(o, s["w"], raw=False)
            for r in s["r"]:
                self._add_dep(o, r, raw=False)
        for r in reads:
            self.slots.setdefault(r, {"w": None, "r": []})["r"].append(o)
        for w in writes:
            s = self.slots[w]
            s["w"] = o
            s["r"] = []
        self.ops[eng].append(o)
        self.all.append(o)
        return o

    def barrier(self):
        return [self.ops[e][-1] for e in ENGS if self.ops[e]]

    def emit(self, nc, sems):
        seq = {id(o): i for i, o in enumerate(self.all)}
        for o in self.all:
            if o.is_dma:
                o.needed = True
            best = {}
            dmas, seen = [], set()
            for d in o.deps:
                if id(d) in seen:
                    continue
                seen.add(id(d))
                if d.is_dma:
                    dmas.append(d)
                elif d.eng not in best or seq[id(best[d.eng])] < seq[id(d)]:
                    best[d.eng] = d
            dd = dmas + list(best.values())
            for d in dd:
                d.needed = True
            o.deps = dd
        cnt = {}
        for o in self.all:
            if o.needed and o.tok is None:
                cnt[o.sem] = cnt.get(o.sem, 0) + o.inc_n
                o.tok = (o.sem, cnt[o.sem])
        self.final_counts = cnt
        stats = {e: [0, 0] for e in ENGS}

        def emit_engine(engobj, ename):
            emitted = {}
            for o in self.ops[ename]:
                for d in o.deps:
                    if d.tok is None:
                        continue
                    sname, v = d.tok
                    if emitted.get(sname, 0) >= v:
                        continue
                    engobj.wait_ge(sems[sname], v)
                    emitted[sname] = v
                    stats[ename][1] += 1
                inst = o.fn(engobj)
                if o.needed:
                    inst.then_inc(sems[o.sem], o.inc_n)
                stats[ename][0] += 1

        with nc.Block("main") as blk:
            blk.tensor(lambda e: emit_engine(e, "tensor"))
            blk.vector(lambda e: emit_engine(e, "vector"))
            blk.scalar(lambda e: emit_engine(e, "scalar"))
            blk.gpsimd(lambda e: emit_engine(e, "gpsimd"))
            blk.sync(lambda e: emit_engine(e, "sync"))
        self.stats = stats


ALIGN = 32


def _al(x):
    return (x + ALIGN - 1) // ALIGN * ALIGN


class KB:
    def __init__(self, dbg=()):
        self.dbg = set(dbg)
        self.nc = bass.Bass("TRN2", target_bir_lowering=False)
        self.stack = ExitStack()
        self.p = Prog()
        self.sems = {}
        self.dbg_names = []

    def sb(self, name, shape, dtype, at_off):
        nc = self.nc
        name = "s_" + name
        h = nc.virtual_sbuf_tensor(name, shape, dtype)
        nc._tensor(h.name, shape, dtype, type="SB")
        mloc = nc.lookup_mloc(h)
        mloc.allocated = True
        mloc.addr = self.arena_base + at_off
        return h

    def build(self):
        nc, st = self.nc, self.stack
        ap = lambda t: t.ap()
        dt = nc.dram_tensor
        self.x_h = ap(dt("x", [BPC, S, D], F32, kind="ExternalInput"))
        self.wq_h = ap(dt("wq", [D, D], F32, kind="ExternalInput"))
        self.bq_h = ap(dt("bq", [D], F32, kind="ExternalInput"))
        self.wk_h = ap(dt("wk", [D, D], F32, kind="ExternalInput"))
        self.bk_h = ap(dt("bk", [D], F32, kind="ExternalInput"))
        self.wv_h = ap(dt("wv", [D, D], F32, kind="ExternalInput"))
        self.bv_h = ap(dt("bv", [D], F32, kind="ExternalInput"))
        self.wm_h = ap(dt("wm", [D, D], F32, kind="ExternalInput"))
        self.bm_h = ap(dt("bm", [D], F32, kind="ExternalInput"))
        self.wqkv_h = ap(dt("wqkv", [D, 3 * D], F32, kind="ExternalInput"))
        self.bqkv_h = ap(dt("bqkv", [3 * D], F32, kind="ExternalInput"))
        self.wg_h = ap(dt("wg", [D, E], F32, kind="ExternalInput"))
        self.bg_h = ap(dt("bg", [E], F32, kind="ExternalInput"))
        self.we_h = ap(dt("we", [E, D, D], F32, kind="ExternalInput"))
        self.be_h = ap(dt("be", [E, D], F32, kind="ExternalInput"))
        self.identb_h = ap(dt("identb", [128, 128], BF16, kind="ExternalInput"))
        self.identf_h = ap(dt("identf", [128, 128], F32, kind="ExternalInput"))
        self.onesb_h = ap(dt("onesb", [1, 128], BF16, kind="ExternalInput"))
        self.masks_h = ap(dt("masks", [128, 4 * 512], BF16, kind="ExternalInput"))
        self.y_h = ap(dt("y", [BPC, S, D], F32, kind="ExternalOutput"))
        pk = "ExternalOutput" if "partial" in self.dbg else "Internal"
        mk = "ExternalOutput" if "moe" in self.dbg else "Internal"
        self.partial_h = ap(dt("partial", [BPC, S, D], F32, kind=pk))
        self.moe_h = ap(dt("moe", [BPC, S, D], F32, kind=mk))

        sem_names = ["mP", "mV", "mA", "mG", "mS"]
        sem_names += [f"qS{i}" for i in range(8)] + [f"qG{i}" for i in range(8)]
        for name in sem_names:
            self.sems[name] = st.enter_context(nc.semaphore(name))

        limit = nc.sbuf_bytes_remaining - 512
        arena = st.enter_context(nc.sbuf_tensor("arena", [128, limit], mybir.dt.uint8))
        self.arena_base = nc.lookup_mloc(arena).addr
        off = [0]

        def place(n):
            a = off[0]
            off[0] = _al(a + n)
            return a

        o_identb = place(256); o_identf = place(512); o_onesb = place(256)
        o_brow_v = place(2048); o_brow_m = place(2048); o_brow_v2 = place(2048)
        o_brow_g = place(64); o_be = place(2048); o_bstage = place(4096)
        o_bqT = place(32); o_bkT = place(32); o_bq2T = place(32); o_bk2T = place(32)
        o_xnT = place(NCH * S * 2)
        o_combine = place(NT * E * 4)
        o_combineT = place(NT * 128 * 2)
        o_xt = place(2 * D * 4)
        o_mo = place(2 * D * 4)
        o_xn = place(2 * D * 2)
        o_small = place(4096)
        o_const = place(64)
        o_wg = place(NCH * E * 2)
        o_masks = place(4 * 512 * 2)
        o_Q = place(131584)
        assert off[0] <= limit, (off[0], limit)
        self.sbuf_used = off[0]

        q = o_Q
        o_kT = q
        o_vbuf = q + 32768
        o_qT = q + 66048
        o_spare = q + 74240
        o_probsT = q + 82432
        o_a1n = q + 90624
        o_outT = q + 98816
        o_acc = q + 100864
        o_wm = q + 109056
        o_wq = q + 125440
        o_we = [q, q + 16384, q + 32768, q + 49152,
                q + 66048, q + 82432, q + 98816, q + 115200]

        sb = self.sb
        self.identb = sb("identb", [128, 128], BF16, o_identb)
        self.identf = sb("identf", [128, 128], F32, o_identf)
        self.onesb = sb("onesb", [1, 128], BF16, o_onesb)
        self.brow_v = sb("brow_v", [1, D], BF16, o_brow_v)
        self.brow_m = sb("brow_m", [1, D], BF16, o_brow_m)
        self.brow_v2 = sb("brow_v2", [1, D], BF16, o_brow_v2)
        self.brow_g = sb("brow_g", [1, E], BF16, o_brow_g)
        self.be = sb("be", [E, D], BF16, o_be)
        self.bstage = sb("bstage", [8, 128], F32, o_bstage)
        self.bqT = sb("bqT", [128, 8], F32, o_bqT)
        self.bkT = sb("bkT", [128, 8], F32, o_bkT)
        self.bq2T = sb("bq2T", [128, 8], F32, o_bq2T)
        self.bk2T = sb("bk2T", [128, 8], F32, o_bk2T)
        self.xnT = sb("xnT", [128, NCH * S], BF16, o_xnT)
        self.combine = sb("combine", [128, NT * E], F32, o_combine)
        self.combineT = sb("combineT", [E, NT * 128], BF16, o_combineT)
        self.xt = sb("xt", [128, 2 * D], F32, o_xt)
        self.mo = sb("mo", [128, 2 * D], F32, o_mo)
        self.xn = sb("xn", [128, 2 * D], BF16, o_xn)
        self.small = sb("small", [128, 1024], F32, o_small)
        self.consts = sb("consts", [128, 2], F32, o_const)
        self.kT = sb("kT", [128, NCH * S], BF16, o_kT)
        self.vbuf = sb("vbuf", [128, NT * H * 65], BF16, o_vbuf)
        self.qTb = sb("qTb", [128, NCH * QG], BF16, o_qT)
        self.a1nT = sb("a1nT", [128, 2 * D], BF16, o_spare)
        self.wgW = sb("wgW", [128, NCH * E], BF16, o_wg)
        self.masks = sb("masks", [128, 4 * 512], BF16, o_masks)
        self.probsT = sb("probsT", [128, 4 * 1024], BF16, o_probsT)
        self.a1n = sb("a1n", [128, 4 * D], BF16, o_a1n)
        self.outT = sb("outT", [65, 2 * QG], BF16, o_outT)
        self.outT2 = sb("outT2", [65, 2 * QG], BF16, o_spare + 4096)
        self.acc = sb("acc", [128, 2 * D], F32, o_acc)
        self.wmW = sb("wmW", [128, NCH * D], BF16, o_wm)
        self.wqW = sb("wqW", [128, 3 * 1024], BF16, o_wq)
        self.weW = [sb(f"weW{e}", [128, NCH * D], BF16, o_we[e]) for e in range(E)]
        self.wkW = sb("wkW", [128, NCH * D], BF16, o_probsT)
        self.wvW = sb("wvW", [128, NCH * D], BF16, o_qT)

        self.PSA = st.enter_context(nc.psum_tensor("PSA", [128, 1024], F32))
        self.PSB = st.enter_context(nc.psum_tensor("PSB", [128, 1024], F32))
        self.PSC = st.enter_context(nc.psum_tensor("PSC", [128, 1024], F32))
        self.PSE1 = st.enter_context(nc.psum_tensor("PSE1", [128, 1024], BF16))
        self.PSE2 = st.enter_context(nc.psum_tensor("PSE2", [128, 1024], BF16))

        nc.const_aps.register(F32, EPS, self.consts[:, 0:1])
        nc.const_aps.register(F32, -CSH, self.consts[:, 1:2])
        self.phase_w0()
        for b in range(BPC):
            self.phase_A(b)
            self.phase_C(b)
            self.phase_B1a(b)
            self.attn(b, causal=True)
            self.phase_B2a(b)
            self.attn(b, causal=False)
        self.phase_fin()
        self.p.emit(self.nc, self.sems)
        return self

    def wmat_load(self, src2d, dst, deps=(), writes=(), tag="w"):
        src = src2d.rearrange("(c p) d -> p c d", p=128)
        d3 = dst[:].rearrange("p (c d) -> p c d", c=NCH)
        return self.p.op("gpsimd", lambda g: g.dma_start(out=d3, in_=src),
                         deps=deps, writes=writes, dma_sem="qG", tag=tag)

    def dbg_dump(self, key, src_ap, shape, dtype, dep_reads):
        if key not in self.dbg:
            return
        name = "dbg_" + key
        h = self.nc.dram_tensor(name, shape, dtype, kind="ExternalOutput").ap()
        self.dbg_names.append(name)
        self.p.op("sync", lambda s: s.dma_start(out=h, in_=src_ap),
                  reads=dep_reads, dma_sem="qS", tag="dbg" + key)

    def phase_w0(self):
        k, p = self, self.p
        g = lambda fn, **kw: p.op("gpsimd", fn, dma_sem="qG", **kw)
        self.ld_identb = g(lambda e: e.dma_start(out=k.identb[:], in_=k.identb_h), writes=("identb",))
        self.ld_identf = g(lambda e: e.dma_start(out=k.identf[:], in_=k.identf_h), writes=("identf",))
        g(lambda e: e.dma_start(out=k.onesb[:], in_=k.onesb_h), writes=("onesb",))
        g(lambda e: e.dma_start(out=k.brow_v[:], in_=k.bv_h[None, :]), writes=("brow_v",))
        g(lambda e: e.dma_start(out=k.brow_m[:], in_=k.bm_h[None, :]), writes=("brow_m",))
        g(lambda e: e.dma_start(out=k.brow_v2[:], in_=k.bqkv_h[None, 2048:3072]), writes=("brow_v2",))
        g(lambda e: e.dma_start(out=k.brow_g[:], in_=k.bg_h[None, :]), writes=("brow_g",))
        g(lambda e: e.dma_start(out=k.be[:], in_=k.be_h), writes=("be",))
        self.wmat_load(k.wg_h, k.wgW, writes=("wgW",), tag="wg")
        g(lambda e: e.dma_start(out=k.masks[:], in_=k.masks_h), writes=("masks",))
        p.op("gpsimd", lambda e: e.memset(k.consts[:, 0:1], EPS), writes=("c_eps",), tag="ceps")
        p.op("gpsimd", lambda e: e.memset(k.consts[:, 1:2], -CSH), writes=("c_csh",), tag="ccsh")
        for nm, src_ap, dstT, wkey in (
                ("bq", k.bq_h[None, :], k.bqT, "bqT"),
                ("bk", k.bk_h[None, :], k.bkT, "bkT"),
                ("bq2", k.bqkv_h[None, 0:1024], k.bq2T, "bq2T"),
                ("bk2", k.bqkv_h[None, 1024:2048], k.bk2T, "bk2T")):
            src8 = src_ap.rearrange("o (c p) -> (o c) p", p=128)
            p.op("sync", lambda e, s=src8: e.dma_start(out=k.bstage[:], in_=s),
                 writes=("bstage",), dma_sem="qS", tag="b" + nm)
            last = p.op("tensor",
                        lambda e: e.transpose(k.PSA[:, 0:8], k.bstage[:],
                                              k.identf[0:8, 0:8]),
                        deps=(self.ld_identf,), reads=("bstage",),
                        writes=("psA0",), tag="trb" + nm)
            p.op("vector", lambda e, d=dstT: e.tensor_copy(d[:], k.PSA[:, 0:8]),
                 deps=(last,), reads=("psA0",), writes=(wkey,), tag="evb" + nm)

    def phase_A(self, b):
        k, p = self, self.p
        sm = k.small

        bar = tuple(p.barrier()) if b > 0 else ()
        qs_lasts = tuple(o for s, o in p.lane_last.items() if s.startswith("qS"))
        self.we_ops = []
        for e in (5, 4, 2, 3, 7, 0, 1, 6):
            o = self.wmat_load(k.we_h[e], k.weW[e],
                               deps=bar + qs_lasts if e == 6 else bar,
                               writes=(f"E{e}",), tag=f"we{e}")
            self.we_ops.append(o)

        def ln_steps(i):
            par = i % 2
            xt = k.xt[:, par * D:(par + 1) * D]
            xnb = k.xn[:, par * D:(par + 1) * D]
            xkey, xnkey = f"xt{par}", f"xn{par}"
            base = 16 * par
            s1, s2 = sm[:, base:base + 1], sm[:, base + 1:base + 2]
            mean, msq = sm[:, base + 2:base + 3], sm[:, base + 3:base + 4]
            var, lnv = sm[:, base + 4:base + 5], sm[:, base + 5:base + 6]
            rstd = sm[:, base + 6:base + 7]
            sq = k.mo[:, par * D:(par + 1) * D]
            sqkey = f"mo{par}"
            sx = f"_{par}"
            yield lambda: p.op("sync", lambda e: e.dma_start(
                out=xt, in_=k.x_h[b, i * 128:(i + 1) * 128, :]),
                writes=(xkey,), dma_sem="qS", tag=f"ldx{i}")
            yield lambda: p.op("scalar", lambda e: e.activation(
                xnb, xt, AF.Copy, bias=0.0, scale=1.0, accum_out=s1),
                reads=(xkey,), writes=(xnkey, "ln_s1" + sx), tag="ln1")
            yield lambda: p.op("scalar", lambda e: e.activation(
                sq, xt, AF.Square, bias=0.0, scale=1.0, accum_out=s2),
                reads=(xkey,), writes=(sqkey, "ln_s2" + sx), tag="ln3")
            yield lambda: p.op("vector", lambda e: e.tensor_scalar_mul(mean, s1, 1.0 / D),
                               reads=("ln_s1" + sx,), writes=("ln_mean" + sx,), tag="ln4")
            yield lambda: p.op("vector", lambda e: e.tensor_mul(msq, mean, mean),
                               reads=("ln_mean" + sx,), writes=("ln_msq" + sx,), tag="ln5")
            yield lambda: p.op("vector", lambda e: e.tensor_scalar_mul(var, s2, 1.0 / D),
                               reads=("ln_s2" + sx,), writes=("ln_var" + sx,), tag="ln6")
            yield lambda: p.op("vector", lambda e: e.tensor_sub(var, var, msq),
                               reads=("ln_msq" + sx, "ln_var" + sx), writes=("ln_var" + sx,), tag="ln7")
            yield lambda: p.op("scalar", lambda e: e.activation(lnv, var, AF.Ln, bias=EPS, scale=1.0),
                               reads=("ln_var" + sx, "c_eps"), writes=("ln_lnv" + sx,), tag="ln8")
            yield lambda: p.op("scalar", lambda e: e.activation(rstd, lnv, AF.Exp, bias=0.0, scale=-0.5),
                               reads=("ln_lnv" + sx,), writes=("ln_rstd" + sx,), tag="ln9")
            yield lambda: p.op("vector", lambda e: e.tensor_scalar(
                xnb, xt, mean[:, 0:1], rstd[:, 0:1], ALU.subtract, ALU.mult),
                reads=(xkey, "ln_mean" + sx, "ln_rstd" + sx), writes=(xnkey,), tag="ln12")
            pse = (k.PSE1, k.PSE2)[par]
            psekey = f"psE{par + 1}"
            def trs():
                last = None
                for c in range(NCH):
                    last = p.op("tensor", lambda e, c=c: e.transpose(
                        pse[:, c * 128:(c + 1) * 128], xnb[:, c * 128:(c + 1) * 128], k.identb[:]),
                        deps=(self.ld_identb,), reads=(xnkey,), writes=(psekey,), tag=f"trxn{i}")
                return last
            yield trs
            dst = k.xnT[:].rearrange("p (c t) -> p c t", c=NCH)[:, :, i * 128:(i + 1) * 128]
            src = pse[:].rearrange("p (c t) -> p c t", c=NCH)
            yield lambda: p.op("vector", lambda e: e.tensor_copy(dst, src),
                               reads=(psekey,), writes=(f"xnT{i}",), tag=f"evxn{i}")

        for i0 in range(0, NT, 2):
            gens = [ln_steps(i0), ln_steps(i0 + 1)]
            done = [False, False]
            while not all(done):
                for gi in range(2):
                    if done[gi]:
                        continue
                    try:
                        next(gens[gi])()
                    except StopIteration:
                        done[gi] = True

        self.dbg_dump(f"xnT{b}", k.xnT[:], [128, NCH * S], BF16,
                      tuple(f"xnT{i}" for i in range(NT)))

        for gi in range(NG):
            gsb = sm[:, 32:64]
            exps = sm[:, 64:96]
            nm0 = sm[:, 96:100]
            ssum, rec = sm[:, 100:104], sm[:, 104:108]
            m1, m2 = sm[:, 108:112], sm[:, 112:116]
            eq, g2, mk2 = sm[:, 116:148], sm[:, 148:180], sm[:, 180:212]
            mmop = None
            for il in range(4):
                i = gi * 4 + il
                col = il * 8
                for c in range(NCH):
                    p.op("tensor", lambda e, c=c, i=i, col=col: e.matmul(
                        k.PSA[:, col:col + 8],
                        k.xnT[:, c * S + i * 128:c * S + (i + 1) * 128],
                        k.wgW[:, c * E:(c + 1) * E],
                        start=(c == 0), stop=False, skip_group_check=True),
                        reads=(f"xnT{i}", "wgW"),
                        writes=("psA0",) if (c == 0 and il == 0) else (), tag=f"gmm{i}")
                mmop = p.op("tensor", lambda e, col=col: e.matmul(
                    k.PSA[:, col:col + 8], k.onesb[:], k.brow_g[:],
                    start=False, stop=True, skip_group_check=True),
                    reads=("onesb", "brow_g"), tag=f"gbias{i}")
            p.op("vector", lambda e, gsb=gsb: e.tensor_copy(gsb, k.PSA[:, 0:32]),
                 deps=(mmop,), reads=("psA0",), writes=("g_gsb",), tag="gc1")
            g3 = gsb.rearrange("p (t e) -> p t e", e=E)
            p.op("vector", lambda e, nm0=nm0, g3=g3: e.reduce_max(nm0, g3, axis=AX.X, negate=True),
                 reads=("g_gsb",), writes=("g_nm0",), tag="gc2")
            aops = []
            for il in range(4):
                sl = slice(il * 8, il * 8 + 8)
                aops.append(p.op("scalar", lambda e, sl=sl, il=il, exps=exps, gsb=gsb, nm0=nm0: e.activation(
                    exps[:, sl], gsb[:, sl], AF.Exp, bias=nm0[:, il:il + 1], scale=1.0),
                    reads=("g_gsb", "g_nm0"), writes=("g_exps",), tag="gexp"))
            e3 = exps.rearrange("p (t e) -> p t e", e=E)
            p.op("vector", lambda e, ssum=ssum, e3=e3: e.reduce_sum(ssum, e3, axis=AX.X),
                 deps=(aops[-1],), reads=("g_exps",), writes=("g_ssum",), tag="gc3")
            p.op("vector", lambda e, rec=rec, ssum=ssum: e.reciprocal(rec, ssum),
                 reads=("g_ssum",), writes=("g_rec",), tag="gc4")
            p.op("vector", lambda e, m1=m1, g3=g3: e.reduce_max(m1, g3, axis=AX.X),
                 reads=("g_gsb",), writes=("g_m1",), tag="gc5")
            for il in range(4):
                sl = slice(il * 8, il * 8 + 8)
                p.op("vector", lambda e, sl=sl, il=il, eq=eq, gsb=gsb, m1=m1: e.tensor_scalar(
                    eq[:, sl], gsb[:, sl], m1[:, il:il + 1], None, ALU.is_equal),
                    reads=("g_gsb", "g_m1"), writes=("g_eq",), tag="geq")
            for il in range(4):
                sl = slice(il * 8, il * 8 + 8)
                p.op("vector", lambda e, sl=sl, g2=g2, eq=eq, gsb=gsb: e.scalar_tensor_tensor(
                    g2[:, sl], eq[:, sl], -1e30, gsb[:, sl], ALU.mult, ALU.add),
                    reads=("g_eq", "g_gsb"), writes=("g_g2",), tag="gg2")
            g23 = g2.rearrange("p (t e) -> p t e", e=E)
            p.op("vector", lambda e, m2=m2, g23=g23: e.reduce_max(m2, g23, axis=AX.X),
                 reads=("g_g2",), writes=("g_m2",), tag="gc6")
            for il in range(4):
                sl = slice(il * 8, il * 8 + 8)
                p.op("vector", lambda e, sl=sl, il=il, mk2=mk2, gsb=gsb, m2=m2: e.tensor_scalar(
                    mk2[:, sl], gsb[:, sl], m2[:, il:il + 1], None, ALU.is_ge),
                    reads=("g_gsb", "g_m2"), writes=("g_mk",), tag="gmk")
            for il in range(4):
                sl = slice(il * 8, il * 8 + 8)
                p.op("vector", lambda e, sl=sl, il=il, exps=exps, rec=rec: e.tensor_scalar(
                    exps[:, sl], exps[:, sl], rec[:, il:il + 1], None, ALU.mult),
                    reads=("g_rec", "g_exps"), writes=("g_exps",), tag="gsm")
            cmb = p.op("vector", lambda e, gi=gi, exps=exps, mk2=mk2: e.tensor_mul(
                k.combine[:, gi * 32:(gi + 1) * 32], exps, mk2),
                reads=("g_exps", "g_mk"), writes=(f"comb{gi}",), tag="gcomb")
            trs = None
            for il in range(4):
                i = gi * 4 + il
                trs = p.op("tensor", lambda e, i=i, il=il: e.transpose(
                    k.PSB[0:8, il * 128:(il + 1) * 128],
                    k.combine[:, i * E:(i + 1) * E], k.identf[:]),
                    deps=(cmb, self.ld_identf), reads=(f"comb{gi}", "identf"),
                    writes=("psB0",), tag=f"trcmb{i}")
            p.op("vector", lambda e, gi=gi: e.tensor_copy(
                k.combineT[:, gi * 512:(gi + 1) * 512], k.PSB[0:8, 0:512]),
                deps=(trs,), reads=("psB0",), writes=(f"combT{gi}",), tag="evcmb")

        self.dbg_dump(f"combine{b}", k.combine[:], [128, NT * E], F32,
                      tuple(f"comb{gi}" for gi in range(NG)))

    EORD = (5, 4, 2, 3, 6, 7, 0, 1)

    def phase_C(self, b):
        k, p = self, self.p
        for i in range(NT):
            gi = i // 4
            par = i % 2
            mot = k.mo[:, par * D:(par + 1) * D]
            mkey = f"mo{par}"
            cb = None
            for h2 in range(2):
                cb = p.op("tensor", lambda e, i=i, h2=h2: e.matmul(
                    k.PSB[:, h2 * 512:(h2 + 1) * 512],
                    k.combineT[:, i * 128:(i + 1) * 128],
                    k.be[:, h2 * 512:(h2 + 1) * 512],
                    start=True, stop=True, skip_group_check=True),
                    reads=(f"combT{gi}", "be"),
                    writes=("psB0", "psB1") if h2 == 0 else (), tag=f"cb{i}")
            prev = p.op("vector", lambda e, mot=mot: e.tensor_copy(mot, k.PSB[:]),
                        deps=(cb,), reads=("psB0", "psB1"), writes=(mkey,), tag=f"evcb{i}")
            for pp, ei in enumerate(self.EORD):
                ps = (k.PSA, k.PSC)[pp % 2]
                pskeys = ("psA0", "psA1") if pp % 2 == 0 else ("psC0", "psC1")
                mm = None
                for c in range(NCH):
                    for h2 in range(2):
                        mm = p.op("tensor", lambda e, c=c, h2=h2, ei=ei, i=i, ps=ps: e.matmul(
                            ps[:, h2 * 512:(h2 + 1) * 512],
                            k.xnT[:, c * S + i * 128:c * S + (i + 1) * 128],
                            k.weW[ei][:, c * D + h2 * 512:c * D + (h2 + 1) * 512],
                            start=(c == 0), stop=(c == NCH - 1), skip_group_check=True),
                            reads=(f"xnT{i}", f"E{ei}"),
                            writes=pskeys if (c == 0 and h2 == 0) else (), tag=f"ex{i}_{ei}")
                prev = p.op("vector", lambda e, mot=mot, ps=ps, i=i, ei=ei: e.scalar_tensor_tensor(
                    mot, ps[:], k.combine[:, i * E + ei:i * E + ei + 1], mot, ALU.mult, ALU.add),
                    deps=(mm, prev), reads=pskeys + (f"comb{gi}", mkey),
                    writes=(mkey,), tag=f"stt{i}_{ei}")
            p.op("sync", lambda e, mot=mot, i=i: e.dma_start(
                out=k.moe_h[b, i * 128:(i + 1) * 128, :], in_=mot),
                reads=(mkey,), dma_sem="qS", tag=f"stmo{i}")

    def phase_B1a(self, b):
        k, p = self, self.p
        self.wmat_load(k.wk_h, k.wkW, writes=("wkW", "wkWb", "E5"), tag="wk")
        self.wmat_load(k.wv_h, k.wvW, writes=("wvW", "wvWb", "E4"), tag="wv")
        self.wmat_load(k.wm_h, k.wmW, writes=("wmW", "E6", "E7"), tag="wm")
        v_ones = k.vbuf[:].rearrange("p (s c) -> p s c", c=65)[:, :, 64:65]
        p.op("gpsimd", lambda e: e.memset(v_ones, 1.0),
             writes=("vones", "E2", "E3"), tag="vones")
        self.proj_kv(b, k.wkW, ("wkW", "wkWb"), k.wvW, ("wvW", "wvWb"),
                     k.bkT, "bkT", k.brow_v, "brow_v")

    def phase_B2a(self, b):
        k, p = self, self.p

        def slot_ops(names):
            dd = []
            for n in names:
                s = p.slots.get(n)
                if s:
                    if s["w"] is not None:
                        dd.append(s["w"])
                    dd.extend(s["r"])
            return tuple(dd)

        def half_load(src2d, dst, half, deps, writes, tag):
            src = src2d[half * 512:(half + 1) * 512, :].rearrange(
                "(c p) d -> p c d", p=128)
            d3 = dst[:, half * 4 * D:(half + 1) * 4 * D].rearrange(
                "p (c d) -> p c d", c=4)
            return p.op("gpsimd", lambda g: g.dma_start(out=d3, in_=src),
                        deps=deps, writes=writes, dma_sem="qG", tag=tag)

        half_load(k.wqkv_h[:, 1024:2048], k.wkW, 0,
                  slot_ops([f"pT{s}" for s in range(4)]), ("wkW",), "wk2a")
        half_load(k.wqkv_h[:, 1024:2048], k.wkW, 1,
                  slot_ops([f"a1n{il}" for il in range(4)]), ("wkWb",), "wk2b")
        half_load(k.wqkv_h[:, 2048:3072], k.wvW, 0,
                  slot_ops([f"qT{d}" for d in range(NCH)]), ("wvW",), "wv2a")
        half_load(k.wqkv_h[:, 2048:3072], k.wvW, 1,
                  slot_ops([f"a1nT{pq}_{c}" for pq in range(2) for c in range(NCH)]
                           + ["outT1"]), ("wvWb",), "wv2b")
        self.proj_kv(b, k.wkW, ("wkW", "wkWb"), k.wvW, ("wvW", "wvWb"),
                     k.bk2T, "bk2T", k.brow_v2, "brow_v2")

    def proj_kv(self, b, wkW, wk_keys, wvW, wv_keys, bkT, bk_key, brow_v, bv_key):
        k, p = self, self.p

        def kt_unit(u):
            d, tseg = u // NG, u % NG
            par = u % 2
            pskey = f"psA{par}"
            pcol = par * 512
            mm = None
            for c in range(NCH):
                mm = p.op("tensor", lambda e, c=c, d=d, tseg=tseg, pcol=pcol: e.matmul(
                    k.PSA[:, pcol:pcol + 512],
                    wkW[:, c * D + d * 128:c * D + (d + 1) * 128],
                    k.xnT[:, c * S + tseg * 512:c * S + (tseg + 1) * 512],
                    start=(c == 0), stop=(c == NCH - 1), skip_group_check=True),
                    reads=(wk_keys[c // 4],) + tuple(f"xnT{i}" for i in range(tseg * 4, tseg * 4 + 4)),
                    writes=(pskey,) if c == 0 else (), tag=f"kt{d}_{tseg}")
            p.op("vector", lambda e, d=d, tseg=tseg, pcol=pcol: e.tensor_scalar(
                k.kT[:, d * S + tseg * 512:d * S + (tseg + 1) * 512],
                k.PSA[:, pcol:pcol + 512], bkT[:, d:d + 1], None, ALU.add),
                deps=(mm,), reads=(pskey, bk_key), writes=("kT",), tag=f"evkt{d}_{tseg}")

        def v_unit(i):
            ps = (k.PSB, k.PSC)[i % 2]
            pskeys = ("psB0", "psB1") if i % 2 == 0 else ("psC0", "psC1")
            for c in range(NCH):
                for h2 in range(2):
                    p.op("tensor", lambda e, c=c, h2=h2, i=i, ps=ps: e.matmul(
                        ps[:, h2 * 512:(h2 + 1) * 512],
                        k.xnT[:, c * S + i * 128:c * S + (i + 1) * 128],
                        wvW[:, c * D + h2 * 512:c * D + (h2 + 1) * 512],
                        start=(c == 0), stop=False, skip_group_check=True),
                        reads=(wv_keys[c // 4], f"xnT{i}"),
                        writes=pskeys if (c == 0 and h2 == 0) else (), tag=f"v{i}")
            mm = None
            for h2 in range(2):
                mm = p.op("tensor", lambda e, h2=h2, ps=ps: e.matmul(
                    ps[:, h2 * 512:(h2 + 1) * 512], k.onesb[:],
                    brow_v[:, h2 * 512:(h2 + 1) * 512],
                    start=False, stop=True, skip_group_check=True),
                    reads=("onesb", bv_key), tag=f"vb{i}")
            dst = k.vbuf[:].rearrange("p (s c) -> p s c", c=65)[:, i * H:(i + 1) * H, 0:64]
            src = ps[:].rearrange("p (s c) -> p s c", c=64)
            p.op("vector", lambda e, dst=dst, src=src: e.tensor_copy(dst, src),
                 deps=(mm,), reads=pskeys, writes=("vbuf",), tag=f"evv{i}")

        for i in range(NT):
            kt_unit(2 * i)
            v_unit(i)
            kt_unit(2 * i + 1)

    def attn(self, b, causal):
        k, p = self, self.p
        wq_src = k.wq_h if causal else k.wqkv_h[:, 0:1024]
        bqT = k.bqT if causal else k.bq2T
        bq_key = "bqT" if causal else "bq2T"

        def emit_epilogue(g):
            for il in range(4):
                i = g * 4 + il
                par = il % 2
                if causal:
                    xop = p.op("sync", lambda e, i=i, par=par: e.dma_start(
                        out=k.xt[:, par * D:(par + 1) * D],
                        in_=k.x_h[b, i * 128:(i + 1) * 128, :]),
                        writes=(f"xt{par}",), dma_sem="qS", tag=f"rlx{i}")
                    for c in range(NCH):
                        pse = (k.PSE1, k.PSE2)[c % 2]
                        psekey = f"psE{c % 2 + 1}"
                        tt = p.op("tensor", lambda e, c=c, il=il, pse=pse: e.transpose(
                            pse[:, 512:640], k.a1n[:, il * D + c * 128:il * D + (c + 1) * 128],
                            k.identb[:]),
                            reads=(f"a1n{il}",), writes=(psekey,), tag=f"tb{i}_{c}")
                        p.op("vector", lambda e, c=c, par=par, pse=pse: e.tensor_copy(
                            k.a1nT[:, par * D + c * 128:par * D + (c + 1) * 128], pse[:, 512:640]),
                            deps=(tt,), reads=(psekey,), writes=(f"a1nT{par}_{c}",), tag=f"evtb{i}_{c}")
                    mm = None
                    for c in range(NCH):
                        for h2 in range(2):
                            mm = p.op("tensor", lambda e, c=c, h2=h2, par=par: e.matmul(
                                k.PSB[:, h2 * 512:(h2 + 1) * 512],
                                k.a1nT[:, par * D + c * 128:par * D + (c + 1) * 128],
                                k.wmW[:, c * D + h2 * 512:c * D + (h2 + 1) * 512],
                                start=(c == 0), stop=False, skip_group_check=True),
                                reads=(f"a1nT{par}_{c}", "wmW"),
                                writes=("psB0", "psB1") if (c == 0 and h2 == 0) else (),
                                tag=f"wm{i}")
                    for h2 in range(2):
                        mm = p.op("tensor", lambda e, h2=h2: e.matmul(
                            k.PSB[:, h2 * 512:(h2 + 1) * 512], k.onesb[:],
                            k.brow_m[:, h2 * 512:(h2 + 1) * 512],
                            start=False, stop=True, skip_group_check=True),
                            reads=("onesb", "brow_m"), tag=f"wmb{i}")
                    p.op("vector", lambda e, par=par: e.tensor_add(
                        k.acc[:, par * D:(par + 1) * D], k.PSB[:],
                        k.xt[:, par * D:(par + 1) * D]),
                        deps=(mm, xop), reads=("psB0", "psB1", f"xt{par}"),
                        writes=(f"acc{par}",), tag=f"pacc{i}")
                    p.op("sync", lambda e, i=i, par=par: e.dma_start(
                        out=k.partial_h[b, i * 128:(i + 1) * 128, :],
                        in_=k.acc[:, par * D:(par + 1) * D]),
                        reads=(f"acc{par}",), dma_sem="qS", tag=f"stp{i}")
                else:
                    pop = p.op("sync", lambda e, i=i, par=par: e.dma_start(
                        out=k.xt[:, par * D:(par + 1) * D],
                        in_=k.partial_h[b, i * 128:(i + 1) * 128, :]),
                        writes=(f"xt{par}",), dma_sem="qS", tag=f"ldp{i}")
                    mop = p.op("sync", lambda e, i=i, par=par: e.dma_start(
                        out=k.mo[:, par * D:(par + 1) * D],
                        in_=k.moe_h[b, i * 128:(i + 1) * 128, :]),
                        writes=(f"mo{par}",), dma_sem="qS", tag=f"ldmo{i}")
                    f1 = p.op("vector", lambda e, par=par, il=il: e.tensor_add(
                        k.acc[:, par * D:(par + 1) * D],
                        k.xt[:, par * D:(par + 1) * D],
                        k.a1n[:, il * D:(il + 1) * D]),
                        deps=(pop,), reads=(f"xt{par}", f"a1n{il}"),
                        writes=(f"acc{par}",), tag=f"fin1_{i}")
                    p.op("vector", lambda e, par=par: e.tensor_add(
                        k.acc[:, par * D:(par + 1) * D],
                        k.acc[:, par * D:(par + 1) * D],
                        k.mo[:, par * D:(par + 1) * D]),
                        deps=(mop, f1), reads=(f"mo{par}", f"acc{par}"),
                        writes=(f"acc{par}",), tag=f"fin2_{i}")
                    self.yst = p.op("sync", lambda e, i=i, par=par: e.dma_start(
                        out=k.y_h[b, i * 128:(i + 1) * 128, :],
                        in_=k.acc[:, par * D:(par + 1) * D]),
                        reads=(f"acc{par}",), dma_sem="qS", tag=f"sty{i}")

        pend_epi = None
        for g in range(NG):
            cmax = 4 * g + 3 if causal else NT - 1
            for d in range(NCH):
                par = d % 3
                ppar = d % 2
                pcol = ppar * 512
                pskey = f"psA{ppar}"
                wsl = k.wqW[:, par * 1024:(par + 1) * 1024]
                wkey = f"wq{par}"
                src = wq_src.rearrange("(c p) d -> p c d", p=128)[:, :, d * 128:(d + 1) * 128]
                d3 = wsl.rearrange("p (c t) -> p c t", c=NCH)
                wop = p.op("gpsimd", lambda e, d3=d3, src=src: e.dma_start(out=d3, in_=src),
                           writes=(wkey,), dma_sem="qG", tag=f"wqd{g}_{d}")
                mm = None
                for c in range(NCH):
                    mm = p.op("tensor", lambda e, c=c, g=g, wsl=wsl, pcol=pcol: e.matmul(
                        k.PSA[:, pcol:pcol + 512],
                        wsl[:, c * 128:(c + 1) * 128],
                        k.xnT[:, c * S + g * QG:c * S + (g + 1) * QG],
                        start=(c == 0), stop=(c == NCH - 1), skip_group_check=True),
                        deps=(wop,),
                        reads=(wkey,) + tuple(f"xnT{i}" for i in range(g * 4, g * 4 + 4)),
                        writes=(pskey,) if c == 0 else (), tag=f"qt{g}_{d}")
                p.op("vector", lambda e, d=d, g=g, bqT=bqT, pcol=pcol: e.tensor_scalar(
                    k.qTb[:, d * QG:(d + 1) * QG], k.PSA[:, pcol:pcol + 512],
                    bqT[:, d:d + 1], None, ALU.add),
                    deps=(mm,), reads=(pskey, bq_key), writes=(f"qT{d}",), tag=f"evqt{g}_{d}")
            if pend_epi is not None:
                emit_epilogue(pend_epi)
                pend_epi = None
            def emit_finish_il(pr, ob, obkey, il):
                hA, hB = 2 * pr, 2 * pr + 1
                if True:
                    par = (pr * 4 + il) % 2
                    pse = (k.PSE1, k.PSE2)[par]
                    psekey = f"psE{par + 1}"
                    p.op("tensor", lambda e, il=il, pse=pse, ob=ob: e.transpose(
                        pse[:, 0:65], ob[0:65, il * 128:(il + 1) * 128], k.identb[0:65, 0:65]),
                        reads=(obkey,), writes=(psekey,), tag=f"t1{g}_{pr}_{il}")
                    t2 = p.op("tensor", lambda e, il=il, pse=pse, ob=ob: e.transpose(
                        pse[:, 128:193], ob[0:65, 512 + il * 128:512 + (il + 1) * 128],
                        k.identb[0:65, 0:65]),
                        reads=(obkey,), writes=(psekey,), tag=f"t2{g}_{pr}_{il}")
                    recA = k.small[:, 8 + par * 2:9 + par * 2]
                    recB = k.small[:, 9 + par * 2:10 + par * 2]
                    r1 = p.op("vector", lambda e, pse=pse, recA=recA: e.reciprocal(recA, pse[:, 64:65]),
                              deps=(t2,), reads=(psekey,), writes=(f"recA{par}",), tag="rcA")
                    r2 = p.op("vector", lambda e, pse=pse, recB=recB: e.reciprocal(recB, pse[:, 192:193]),
                              reads=(psekey,), writes=(f"recB{par}",), tag="rcB")
                    dstA = k.a1n[:, il * D + hA * 64: il * D + hA * 64 + 64]
                    dstB = k.a1n[:, il * D + hB * 64: il * D + hB * 64 + 64]
                    p.op("vector", lambda e, pse=pse, dstA=dstA, recA=recA: e.tensor_scalar(
                        dstA, pse[:, 0:64], recA[:, 0:1], None, ALU.mult),
                        deps=(r1,), reads=(psekey, f"recA{par}"),
                        writes=(f"a1n{il}",), tag="nrmA")
                    p.op("vector", lambda e, pse=pse, dstB=dstB, recB=recB: e.tensor_scalar(
                        dstB, pse[:, 128:192], recB[:, 0:1], None, ALU.mult),
                        deps=(r2,), reads=(psekey, f"recB{par}"),
                        writes=(f"a1n{il}",), tag="nrmB")
            def finish_steps(pr, ob, obkey):
                for il in range(4):
                    emit_finish_il(pr, ob, obkey, il)
                    yield

            def pair_steps(pr):
                hA, hB = 2 * pr, 2 * pr + 1
                av_last = None
                pend = []
                def emit_av(dep_probs, c, slot):
                    nonlocal av_last
                    p.op("tensor", lambda e, c=c, hA=hA, slot=slot, cmax=cmax: e.matmul(
                        k.PSC[0:65, 0:512],
                        k.vbuf[:, (c * H + hA) * 65:(c * H + hA) * 65 + 65],
                        k.probsT[:, slot * 1024:slot * 1024 + 512],
                        start=(c == 0), stop=(c == cmax), skip_group_check=True),
                        deps=(dep_probs,), reads=("vbuf", "vones", f"pT{slot}"),
                        writes=("psC0",) if c == 0 else (), tag=f"avA{g}_{pr}_{c}")
                    av_last = p.op("tensor", lambda e, c=c, hB=hB, slot=slot, cmax=cmax: e.matmul(
                        k.PSC[0:65, 512:1024],
                        k.vbuf[:, (c * H + hB) * 65:(c * H + hB) * 65 + 65],
                        k.probsT[:, slot * 1024 + 512:slot * 1024 + 1024],
                        start=(c == 0), stop=(c == cmax), skip_group_check=True),
                        deps=(dep_probs,), reads=("vbuf", "vones", f"pT{slot}"),
                        writes=("psC1",) if c == 0 else (), tag=f"avB{g}_{pr}_{c}")
                for c in range(cmax + 1):
                    yield
                    sps = (k.PSA, k.PSB)[c % 2]
                    skeys = ("psA0", "psA1") if c % 2 == 0 else ("psB0", "psB1")
                    slot = c % 4
                    p.op("tensor", lambda e, c=c, pr=pr, sps=sps: e.matmul(
                        sps[:, 0:512],
                        k.kT[0:64, pr * S + c * 128:pr * S + (c + 1) * 128],
                        k.qTb[0:64, pr * QG:(pr + 1) * QG],
                        start=True, stop=True, skip_group_check=True,
                        tile_position=(0, 0)),
                        reads=("kT", f"qT{pr}"), writes=(skeys[0],), tag=f"scA{g}_{pr}_{c}")
                    sB = p.op("tensor", lambda e, c=c, pr=pr, sps=sps: e.matmul(
                        sps[:, 512:1024],
                        k.kT[64:128, pr * S + c * 128:pr * S + (c + 1) * 128],
                        k.qTb[64:128, pr * QG:(pr + 1) * QG],
                        start=True, stop=True, skip_group_check=True,
                        tile_position=(64, 0)),
                        reads=("kT", f"qT{pr}"), writes=(skeys[1],), tag=f"scB{g}_{pr}_{c}")
                    ex = p.op("scalar", lambda e, slot=slot, sps=sps: e.activation(
                        k.probsT[:, slot * 1024:(slot + 1) * 1024], sps[:],
                        AF.Exp, bias=-CSH, scale=0.125),
                        deps=(sB,), reads=skeys + ("c_csh",), writes=(f"pT{slot}",), tag=f"exp{g}_{pr}_{c}")
                    dep_probs = ex
                    if causal and c >= 4 * g:
                        j = c - 4 * g
                        msl = k.masks[:, j * 512:(j + 1) * 512]
                        for hh in range(2):
                            sl = k.probsT[:, slot * 1024 + hh * 512: slot * 1024 + hh * 512 + 512]
                            dep_probs = p.op("vector", lambda e, sl=sl, msl=msl: e.tensor_mul(
                                sl, sl, msl),
                                deps=(dep_probs,), reads=(f"pT{slot}", "masks"),
                                writes=(f"pT{slot}",), tag=f"msk{g}_{pr}_{c}")
                    pend.append((dep_probs, c, slot))
                    if c % 2 == 1:
                        while len(pend) > 2:
                            emit_av(*pend.pop(0))
                while pend:
                    emit_av(*pend.pop(0))
                ob = (k.outT, k.outT2)[pr % 2]
                obkey = f"outT{pr % 2}"
                p.op("vector", lambda e, ob=ob: e.tensor_copy(ob[0:65, :], k.PSC[0:65, :]),
                     deps=(av_last,), reads=("psC0", "psC1"),
                     writes=(obkey,), tag=f"evout{g}_{pr}")

            fin_gen = None
            for pr in range(NPAIR):
                pg = pair_steps(pr)
                for _ in pg:
                    if fin_gen is not None:
                        next(fin_gen, None)
                while fin_gen is not None and next(fin_gen, StopIteration) is not StopIteration:
                    pass
                fin_gen = finish_steps(pr, (k.outT, k.outT2)[pr % 2], f"outT{pr % 2}")
            while next(fin_gen, StopIteration) is not StopIteration:
                pass
            pend_epi = g
        emit_epilogue(pend_epi)

    def phase_fin(self):
        p = self.p
        sync_lasts = [o for s, o in p.lane_last.items() if s.startswith("qS")]
        gp_lasts = [o for s, o in p.lane_last.items() if s.startswith("qG")]
        p.op("sync", lambda e: e.nop(), deps=[self.yst] + sync_lasts, tag="finS")
        p.op("gpsimd", lambda e: e.nop(), deps=gp_lasts, tag="finG")


def build_program(dbg=()):
    kb = KB(dbg)
    kb.build()
    return kb


def _make_masks():
    k = np.arange(128)[:, None]
    q = np.arange(512)[None, :]
    cols = [((q - k) >= 128 * j).astype(np.float32) for j in range(4)]
    return np.concatenate(cols, axis=1).astype(ml_dtypes.bfloat16)


def make_in_maps(inputs):
    x = np.asarray(inputs["L0_attn_x"])
    common = {
        "wq": np.asarray(inputs["L0_attn_Wq"]), "bq": np.asarray(inputs["L0_attn_bq"]),
        "wk": np.asarray(inputs["L0_attn_Wk"]), "bk": np.asarray(inputs["L0_attn_bk"]),
        "wv": np.asarray(inputs["L0_attn_Wv"]), "bv": np.asarray(inputs["L0_attn_bv"]),
        "wm": np.asarray(inputs["L0_attn_Wm"]), "bm": np.asarray(inputs["L0_attn_bm"]),
        "wqkv": np.asarray(inputs["L0_ffn_W_qkv"]), "bqkv": np.asarray(inputs["L0_ffn_b_qkv"]),
        "wg": np.asarray(inputs["L0_ffn_W_gate"]), "bg": np.asarray(inputs["L0_ffn_b_gate"]),
        "we": np.asarray(inputs["L0_ffn_W_experts"]), "be": np.asarray(inputs["L0_ffn_b_experts"]),
        "identb": np.eye(128, dtype=np.float32).astype(ml_dtypes.bfloat16),
        "identf": np.eye(128, dtype=np.float32),
        "onesb": np.ones((1, 128), dtype=np.float32).astype(ml_dtypes.bfloat16),
        "masks": _make_masks(),
    }
    in_maps = []
    for core in range(8):
        m = dict(common)
        m["x"] = np.ascontiguousarray(x[core * BPC:(core + 1) * BPC])
        in_maps.append(m)
    return in_maps


def kernel(**inputs):
    kb = build_program()
    in_maps = make_in_maps(inputs)
    res = run_bass_kernel_spmd(kb.nc, in_maps, core_ids=list(range(8)))
    return np.concatenate([r["y"] for r in res.results], axis=0)


# version 35
# speedup vs baseline: 1.1093x; 1.1093x over previous
import numpy as np
import ml_dtypes
from contextlib import ExitStack

import concourse.bass as bass
import concourse.mybir as mybir
from concourse.bass_utils import run_bass_kernel_spmd

F32 = mybir.dt.float32
BF16 = mybir.dt.bfloat16
AF = mybir.ActivationFunctionType
ALU = mybir.AluOpType
AX = mybir.AxisListType

BPC = 2
S = 2048
D = 1024
H = 16
E = 8
NT = S // 128
NCH = D // 128
NG = 4
QG = 512
NPAIR = 8
CSH = 20.0
EPS = 1e-5

ENGS = ("tensor", "vector", "scalar", "gpsimd", "sync")


class Op:
    __slots__ = ("eng", "fn", "deps", "sem", "inc_n", "tok", "needed", "tag", "is_dma")

    def __init__(self, eng, fn, sem, inc_n, tag, is_dma):
        self.eng = eng
        self.fn = fn
        self.deps = []
        self.sem = sem
        self.inc_n = inc_n
        self.tok = None
        self.needed = False
        self.tag = tag
        self.is_dma = is_dma


NLANES = 8


class Prog:
    def __init__(self):
        self.ops = {e: [] for e in ENGS}
        self.all = []
        self.slots = {}
        self.eng_sem = {"tensor": "mP", "vector": "mV", "scalar": "mA",
                        "gpsimd": "mG", "sync": "mS"}
        self.qcnt = {"qS": 0, "qG": 0}
        self.lane_last = {}

    def _add_dep(self, o, d, raw):
        if d is None or d is o:
            return
        same = d.eng == o.eng and not (d.is_dma or o.is_dma)
        if same and (not raw or o.eng == "tensor"):
            return
        o.deps.append(d)

    def op(self, eng, fn, deps=(), reads=(), writes=(), dma_sem=None, tag=""):
        is_dma = dma_sem is not None
        if is_dma:
            n = self.qcnt[dma_sem]
            self.qcnt[dma_sem] = n + 1
            lane = n % NLANES
            sem = f"{dma_sem}{lane}"
            o = Op(eng, fn, sem, 16, tag, True)
            o.tok = (sem, 16 * (n // NLANES + 1))
            prev = self.lane_last.get(sem)
            if prev is not None:
                o.deps.append(prev)
            self.lane_last[sem] = o
        else:
            o = Op(eng, fn, self.eng_sem[eng], 1, tag, False)
        for d in deps:
            self._add_dep(o, d, raw=True)
        for r in reads:
            s = self.slots.get(r)
            if s and s["w"] is not None:
                self._add_dep(o, s["w"], raw=True)
        for w in writes:
            s = self.slots.setdefault(w, {"w": None, "r": []})
            self._add_dep(o, s["w"], raw=False)
            for r in s["r"]:
                self._add_dep(o, r, raw=False)
        for r in reads:
            self.slots.setdefault(r, {"w": None, "r": []})["r"].append(o)
        for w in writes:
            s = self.slots[w]
            s["w"] = o
            s["r"] = []
        self.ops[eng].append(o)
        self.all.append(o)
        return o

    def barrier(self):
        return [self.ops[e][-1] for e in ENGS if self.ops[e]]

    def emit(self, nc, sems):
        seq = {id(o): i for i, o in enumerate(self.all)}
        for o in self.all:
            if o.is_dma:
                o.needed = True
            best = {}
            dmas, seen = [], set()
            for d in o.deps:
                if id(d) in seen:
                    continue
                seen.add(id(d))
                if d.is_dma:
                    dmas.append(d)
                elif d.eng not in best or seq[id(best[d.eng])] < seq[id(d)]:
                    best[d.eng] = d
            dd = dmas + list(best.values())
            for d in dd:
                d.needed = True
            o.deps = dd
        cnt = {}
        for o in self.all:
            if o.needed and o.tok is None:
                cnt[o.sem] = cnt.get(o.sem, 0) + o.inc_n
                o.tok = (o.sem, cnt[o.sem])
        self.final_counts = cnt
        stats = {e: [0, 0] for e in ENGS}

        def emit_engine(engobj, ename):
            emitted = {}
            for o in self.ops[ename]:
                for d in o.deps:
                    if d.tok is None:
                        continue
                    sname, v = d.tok
                    if emitted.get(sname, 0) >= v:
                        continue
                    engobj.wait_ge(sems[sname], v)
                    emitted[sname] = v
                    stats[ename][1] += 1
                inst = o.fn(engobj)
                if o.needed:
                    inst.then_inc(sems[o.sem], o.inc_n)
                stats[ename][0] += 1

        with nc.Block("main") as blk:
            blk.tensor(lambda e: emit_engine(e, "tensor"))
            blk.vector(lambda e: emit_engine(e, "vector"))
            blk.scalar(lambda e: emit_engine(e, "scalar"))
            blk.gpsimd(lambda e: emit_engine(e, "gpsimd"))
            blk.sync(lambda e: emit_engine(e, "sync"))
        self.stats = stats


ALIGN = 32


def _al(x):
    return (x + ALIGN - 1) // ALIGN * ALIGN


class KB:
    def __init__(self, dbg=()):
        self.dbg = set(dbg)
        self.nc = bass.Bass("TRN2", target_bir_lowering=False)
        self.stack = ExitStack()
        self.p = Prog()
        self.sems = {}
        self.dbg_names = []

    def sb(self, name, shape, dtype, at_off):
        nc = self.nc
        name = "s_" + name
        h = nc.virtual_sbuf_tensor(name, shape, dtype)
        nc._tensor(h.name, shape, dtype, type="SB")
        mloc = nc.lookup_mloc(h)
        mloc.allocated = True
        mloc.addr = self.arena_base + at_off
        return h

    def build(self):
        nc, st = self.nc, self.stack
        ap = lambda t: t.ap()
        dt = nc.dram_tensor
        self.x_h = ap(dt("x", [BPC, S, D], F32, kind="ExternalInput"))
        self.wq_h = ap(dt("wq", [D, D], F32, kind="ExternalInput"))
        self.bq_h = ap(dt("bq", [D], F32, kind="ExternalInput"))
        self.wk_h = ap(dt("wk", [D, D], F32, kind="ExternalInput"))
        self.bk_h = ap(dt("bk", [D], F32, kind="ExternalInput"))
        self.wv_h = ap(dt("wv", [D, D], F32, kind="ExternalInput"))
        self.bv_h = ap(dt("bv", [D], F32, kind="ExternalInput"))
        self.wm_h = ap(dt("wm", [D, D], F32, kind="ExternalInput"))
        self.bm_h = ap(dt("bm", [D], F32, kind="ExternalInput"))
        self.wqkv_h = ap(dt("wqkv", [D, 3 * D], F32, kind="ExternalInput"))
        self.bqkv_h = ap(dt("bqkv", [3 * D], F32, kind="ExternalInput"))
        self.wg_h = ap(dt("wg", [D, E], F32, kind="ExternalInput"))
        self.bg_h = ap(dt("bg", [E], F32, kind="ExternalInput"))
        self.we_h = ap(dt("we", [E, D, D], F32, kind="ExternalInput"))
        self.be_h = ap(dt("be", [E, D], F32, kind="ExternalInput"))
        self.identb_h = ap(dt("identb", [128, 128], BF16, kind="ExternalInput"))
        self.identf_h = ap(dt("identf", [128, 128], F32, kind="ExternalInput"))
        self.onesb_h = ap(dt("onesb", [1, 128], BF16, kind="ExternalInput"))
        self.masks_h = ap(dt("masks", [128, 4 * 512], BF16, kind="ExternalInput"))
        self.y_h = ap(dt("y", [BPC, S, D], F32, kind="ExternalOutput"))
        pk = "ExternalOutput" if "partial" in self.dbg else "Internal"
        mk = "ExternalOutput" if "moe" in self.dbg else "Internal"
        self.partial_h = ap(dt("partial", [BPC, S, D], F32, kind=pk))
        self.moe_h = ap(dt("moe", [BPC, S, D], F32, kind=mk))

        sem_names = ["mP", "mV", "mA", "mG", "mS"]
        sem_names += [f"qS{i}" for i in range(8)] + [f"qG{i}" for i in range(8)]
        for name in sem_names:
            self.sems[name] = st.enter_context(nc.semaphore(name))

        limit = nc.sbuf_bytes_remaining - 512
        arena = st.enter_context(nc.sbuf_tensor("arena", [128, limit], mybir.dt.uint8))
        self.arena_base = nc.lookup_mloc(arena).addr
        off = [0]

        def place(n):
            a = off[0]
            off[0] = _al(a + n)
            return a

        o_identb = place(256); o_identf = place(512); o_onesb = place(256)
        o_brow_v = place(2048); o_brow_m = place(2048); o_brow_v2 = place(2048)
        o_brow_g = place(64); o_be = place(2048); o_bstage = place(4096)
        o_bqT = place(32); o_bkT = place(32); o_bq2T = place(32); o_bk2T = place(32)
        o_xnT = place(NCH * S * 2)
        o_combine = place(NT * E * 4)
        o_combineT = place(NT * 128 * 2)
        o_xt = place(2 * D * 4)
        o_mo = place(2 * D * 4)
        o_xn = place(2 * D * 2)
        o_small = place(4096)
        o_const = place(64)
        o_wg = place(NCH * E * 2)
        o_masks = place(4 * 512 * 2)
        o_Q = place(131584)
        assert off[0] <= limit, (off[0], limit)
        self.sbuf_used = off[0]

        q = o_Q
        o_kT = q
        o_vbuf = q + 32768
        o_qT = q + 66048
        o_spare = q + 74240
        o_probsT = q + 82432
        o_a1n = q + 90624
        o_outT = q + 98816
        o_acc = q + 100864
        o_wm = q + 109056
        o_wq = q + 125440
        o_we = [q, q + 16384, q + 32768, q + 49152,
                q + 66048, q + 82432, q + 98816, q + 115200]

        sb = self.sb
        self.identb = sb("identb", [128, 128], BF16, o_identb)
        self.identf = sb("identf", [128, 128], F32, o_identf)
        self.onesb = sb("onesb", [1, 128], BF16, o_onesb)
        self.brow_v = sb("brow_v", [1, D], BF16, o_brow_v)
        self.brow_m = sb("brow_m", [1, D], BF16, o_brow_m)
        self.brow_v2 = sb("brow_v2", [1, D], BF16, o_brow_v2)
        self.brow_g = sb("brow_g", [1, E], BF16, o_brow_g)
        self.be = sb("be", [E, D], BF16, o_be)
        self.bstage = sb("bstage", [8, 128], F32, o_bstage)
        self.bqT = sb("bqT", [128, 8], F32, o_bqT)
        self.bkT = sb("bkT", [128, 8], F32, o_bkT)
        self.bq2T = sb("bq2T", [128, 8], F32, o_bq2T)
        self.bk2T = sb("bk2T", [128, 8], F32, o_bk2T)
        self.xnT = sb("xnT", [128, NCH * S], BF16, o_xnT)
        self.combine = sb("combine", [128, NT * E], F32, o_combine)
        self.combineT = sb("combineT", [E, NT * 128], BF16, o_combineT)
        self.xt = sb("xt", [128, 2 * D], F32, o_xt)
        self.mo = sb("mo", [128, 2 * D], F32, o_mo)
        self.xn = sb("xn", [128, 2 * D], BF16, o_xn)
        self.small = sb("small", [128, 1024], F32, o_small)
        self.consts = sb("consts", [128, 2], F32, o_const)
        self.kT = sb("kT", [128, NCH * S], BF16, o_kT)
        self.vbuf = sb("vbuf", [128, NT * H * 65], BF16, o_vbuf)
        self.qTb = sb("qTb", [128, NCH * QG], BF16, o_qT)
        self.a1nT = sb("a1nT", [128, 2 * D], BF16, o_spare)
        self.wgW = sb("wgW", [128, NCH * E], BF16, o_wg)
        self.masks = sb("masks", [128, 4 * 512], BF16, o_masks)
        self.probsT = sb("probsT", [128, 4 * 1024], BF16, o_probsT)
        self.a1n = sb("a1n", [128, 4 * D], BF16, o_a1n)
        self.outT = sb("outT", [65, 2 * QG], BF16, o_outT)
        self.outT2 = sb("outT2", [65, 2 * QG], BF16, o_spare + 4096)
        self.acc = sb("acc", [128, 2 * D], F32, o_acc)
        self.wmW = sb("wmW", [128, NCH * D], BF16, o_wm)
        self.wqW = sb("wqW", [128, 3 * 1024], BF16, o_wq)
        self.weW = [sb(f"weW{e}", [128, NCH * D], BF16, o_we[e]) for e in range(E)]
        self.wkW = sb("wkW", [128, NCH * D], BF16, o_probsT)
        self.wvW = sb("wvW", [128, NCH * D], BF16, o_qT)

        self.PSA = st.enter_context(nc.psum_tensor("PSA", [128, 1024], F32))
        self.PSB = st.enter_context(nc.psum_tensor("PSB", [128, 1024], F32))
        self.PSC = st.enter_context(nc.psum_tensor("PSC", [128, 1024], F32))
        self.PSE1 = st.enter_context(nc.psum_tensor("PSE1", [128, 1024], BF16))
        self.PSE2 = st.enter_context(nc.psum_tensor("PSE2", [128, 1024], BF16))

        nc.const_aps.register(F32, EPS, self.consts[:, 0:1])
        nc.const_aps.register(F32, -CSH, self.consts[:, 1:2])
        self.phase_w0()
        for b in range(BPC):
            self.phase_A(b)
            self.phase_C(b)
            self.phase_B1a(b)
            self.attn(b, causal=True)
            self.phase_B2a(b)
            self.attn(b, causal=False)
        self.phase_fin()
        self.p.emit(self.nc, self.sems)
        return self

    def wmat_load(self, src2d, dst, deps=(), writes=(), tag="w"):
        src = src2d.rearrange("(c p) d -> p c d", p=128)
        d3 = dst[:].rearrange("p (c d) -> p c d", c=NCH)
        return self.p.op("gpsimd", lambda g: g.dma_start(out=d3, in_=src),
                         deps=deps, writes=writes, dma_sem="qG", tag=tag)

    def dbg_dump(self, key, src_ap, shape, dtype, dep_reads):
        if key not in self.dbg:
            return
        name = "dbg_" + key
        h = self.nc.dram_tensor(name, shape, dtype, kind="ExternalOutput").ap()
        self.dbg_names.append(name)
        self.p.op("sync", lambda s: s.dma_start(out=h, in_=src_ap),
                  reads=dep_reads, dma_sem="qS", tag="dbg" + key)

    def phase_w0(self):
        k, p = self, self.p
        g = lambda fn, **kw: p.op("gpsimd", fn, dma_sem="qG", **kw)
        self.ld_identb = g(lambda e: e.dma_start(out=k.identb[:], in_=k.identb_h), writes=("identb",))
        self.ld_identf = g(lambda e: e.dma_start(out=k.identf[:], in_=k.identf_h), writes=("identf",))
        g(lambda e: e.dma_start(out=k.onesb[:], in_=k.onesb_h), writes=("onesb",))
        g(lambda e: e.dma_start(out=k.brow_v[:], in_=k.bv_h[None, :]), writes=("brow_v",))
        g(lambda e: e.dma_start(out=k.brow_m[:], in_=k.bm_h[None, :]), writes=("brow_m",))
        g(lambda e: e.dma_start(out=k.brow_v2[:], in_=k.bqkv_h[None, 2048:3072]), writes=("brow_v2",))
        g(lambda e: e.dma_start(out=k.brow_g[:], in_=k.bg_h[None, :]), writes=("brow_g",))
        g(lambda e: e.dma_start(out=k.be[:], in_=k.be_h), writes=("be",))
        self.wmat_load(k.wg_h, k.wgW, writes=("wgW",), tag="wg")
        g(lambda e: e.dma_start(out=k.masks[:], in_=k.masks_h), writes=("masks",))
        p.op("gpsimd", lambda e: e.memset(k.consts[:, 0:1], EPS), writes=("c_eps",), tag="ceps")
        p.op("gpsimd", lambda e: e.memset(k.consts[:, 1:2], -CSH), writes=("c_csh",), tag="ccsh")
        for nm, src_ap, dstT, wkey in (
                ("bq", k.bq_h[None, :], k.bqT, "bqT"),
                ("bk", k.bk_h[None, :], k.bkT, "bkT"),
                ("bq2", k.bqkv_h[None, 0:1024], k.bq2T, "bq2T"),
                ("bk2", k.bqkv_h[None, 1024:2048], k.bk2T, "bk2T")):
            src8 = src_ap.rearrange("o (c p) -> (o c) p", p=128)
            p.op("sync", lambda e, s=src8: e.dma_start(out=k.bstage[:], in_=s),
                 writes=("bstage",), dma_sem="qS", tag="b" + nm)
            last = p.op("tensor",
                        lambda e: e.transpose(k.PSA[:, 0:8], k.bstage[:],
                                              k.identf[0:8, 0:8]),
                        deps=(self.ld_identf,), reads=("bstage",),
                        writes=("psA0",), tag="trb" + nm)
            p.op("vector", lambda e, d=dstT: e.tensor_copy(d[:], k.PSA[:, 0:8]),
                 deps=(last,), reads=("psA0",), writes=(wkey,), tag="evb" + nm)

    def phase_A(self, b):
        k, p = self, self.p
        sm = k.small

        bar = tuple(p.barrier()) if b > 0 else ()
        qs_lasts = tuple(o for s, o in p.lane_last.items() if s.startswith("qS"))
        self.we_ops = []
        for e in (5, 4, 2, 3, 7, 0, 1, 6):
            o = self.wmat_load(k.we_h[e], k.weW[e],
                               deps=bar + qs_lasts if e == 6 else bar,
                               writes=(f"E{e}",), tag=f"we{e}")
            self.we_ops.append(o)

        def ln_steps(i):
            par = i % 2
            xt = k.xt[:, par * D:(par + 1) * D]
            xnb = k.xn[:, par * D:(par + 1) * D]
            xkey, xnkey = f"xt{par}", f"xn{par}"
            base = 16 * par
            s1, s2 = sm[:, base:base + 1], sm[:, base + 1:base + 2]
            mean, msq = sm[:, base + 2:base + 3], sm[:, base + 3:base + 4]
            var, lnv = sm[:, base + 4:base + 5], sm[:, base + 5:base + 6]
            rstd = sm[:, base + 6:base + 7]
            sq = k.mo[:, par * D:(par + 1) * D]
            sqkey = f"mo{par}"
            sx = f"_{par}"
            yield lambda: p.op("sync", lambda e: e.dma_start(
                out=xt, in_=k.x_h[b, i * 128:(i + 1) * 128, :]),
                writes=(xkey,), dma_sem="qS", tag=f"ldx{i}")
            yield lambda: p.op("scalar", lambda e: e.activation(
                xnb, xt, AF.Copy, bias=0.0, scale=1.0, accum_out=s1),
                reads=(xkey,), writes=(xnkey, "ln_s1" + sx), tag="ln1")
            yield lambda: p.op("scalar", lambda e: e.activation(
                sq, xt, AF.Square, bias=0.0, scale=1.0, accum_out=s2),
                reads=(xkey,), writes=(sqkey, "ln_s2" + sx), tag="ln3")
            yield lambda: p.op("vector", lambda e: e.tensor_scalar_mul(mean, s1, 1.0 / D),
                               reads=("ln_s1" + sx,), writes=("ln_mean" + sx,), tag="ln4")
            yield lambda: p.op("vector", lambda e: e.tensor_mul(msq, mean, mean),
                               reads=("ln_mean" + sx,), writes=("ln_msq" + sx,), tag="ln5")
            yield lambda: p.op("vector", lambda e: e.tensor_scalar_mul(var, s2, 1.0 / D),
                               reads=("ln_s2" + sx,), writes=("ln_var" + sx,), tag="ln6")
            yield lambda: p.op("vector", lambda e: e.tensor_sub(var, var, msq),
                               reads=("ln_msq" + sx, "ln_var" + sx), writes=("ln_var" + sx,), tag="ln7")
            yield lambda: p.op("scalar", lambda e: e.activation(lnv, var, AF.Ln, bias=EPS, scale=1.0),
                               reads=("ln_var" + sx, "c_eps"), writes=("ln_lnv" + sx,), tag="ln8")
            yield lambda: p.op("scalar", lambda e: e.activation(rstd, lnv, AF.Exp, bias=0.0, scale=-0.5),
                               reads=("ln_lnv" + sx,), writes=("ln_rstd" + sx,), tag="ln9")
            yield lambda: p.op("vector", lambda e: e.tensor_scalar(
                xnb, xt, mean[:, 0:1], rstd[:, 0:1], ALU.subtract, ALU.mult),
                reads=(xkey, "ln_mean" + sx, "ln_rstd" + sx), writes=(xnkey,), tag="ln12")
            pse = (k.PSE1, k.PSE2)[par]
            psekey = f"psE{par + 1}"
            def trs():
                last = None
                for c in range(NCH):
                    last = p.op("tensor", lambda e, c=c: e.transpose(
                        pse[:, c * 128:(c + 1) * 128], xnb[:, c * 128:(c + 1) * 128], k.identb[:]),
                        deps=(self.ld_identb,), reads=(xnkey,), writes=(psekey,), tag=f"trxn{i}")
                return last
            yield trs
            dst = k.xnT[:].rearrange("p (c t) -> p c t", c=NCH)[:, :, i * 128:(i + 1) * 128]
            src = pse[:].rearrange("p (c t) -> p c t", c=NCH)
            yield lambda: p.op("vector", lambda e: e.tensor_copy(dst, src),
                               reads=(psekey,), writes=(f"xnT{i}",), tag=f"evxn{i}")

        for i0 in range(0, NT, 2):
            gens = [ln_steps(i0), ln_steps(i0 + 1)]
            done = [False, False]
            while not all(done):
                for gi in range(2):
                    if done[gi]:
                        continue
                    try:
                        next(gens[gi])()
                    except StopIteration:
                        done[gi] = True

        self.dbg_dump(f"xnT{b}", k.xnT[:], [128, NCH * S], BF16,
                      tuple(f"xnT{i}" for i in range(NT)))

        for gi in range(NG):
            gsb = sm[:, 32:64]
            exps = sm[:, 64:96]
            nm0 = sm[:, 96:100]
            ssum, rec = sm[:, 100:104], sm[:, 104:108]
            m1, m2 = sm[:, 108:112], sm[:, 112:116]
            eq, g2, mk2 = sm[:, 116:148], sm[:, 148:180], sm[:, 180:212]
            mmop = None
            for il in range(4):
                i = gi * 4 + il
                col = il * 8
                for c in range(NCH):
                    p.op("tensor", lambda e, c=c, i=i, col=col: e.matmul(
                        k.PSA[:, col:col + 8],
                        k.xnT[:, c * S + i * 128:c * S + (i + 1) * 128],
                        k.wgW[:, c * E:(c + 1) * E],
                        start=(c == 0), stop=False, skip_group_check=True),
                        reads=(f"xnT{i}", "wgW"),
                        writes=("psA0",) if (c == 0 and il == 0) else (), tag=f"gmm{i}")
                mmop = p.op("tensor", lambda e, col=col: e.matmul(
                    k.PSA[:, col:col + 8], k.onesb[:], k.brow_g[:],
                    start=False, stop=True, skip_group_check=True),
                    reads=("onesb", "brow_g"), tag=f"gbias{i}")
            p.op("vector", lambda e, gsb=gsb: e.tensor_copy(gsb, k.PSA[:, 0:32]),
                 deps=(mmop,), reads=("psA0",), writes=("g_gsb",), tag="gc1")
            g3 = gsb.rearrange("p (t e) -> p t e", e=E)
            p.op("vector", lambda e, nm0=nm0, g3=g3: e.reduce_max(nm0, g3, axis=AX.X, negate=True),
                 reads=("g_gsb",), writes=("g_nm0",), tag="gc2")
            aops = []
            for il in range(4):
                sl = slice(il * 8, il * 8 + 8)
                aops.append(p.op("scalar", lambda e, sl=sl, il=il, exps=exps, gsb=gsb, nm0=nm0: e.activation(
                    exps[:, sl], gsb[:, sl], AF.Exp, bias=nm0[:, il:il + 1], scale=1.0),
                    reads=("g_gsb", "g_nm0"), writes=("g_exps",), tag="gexp"))
            e3 = exps.rearrange("p (t e) -> p t e", e=E)
            p.op("vector", lambda e, ssum=ssum, e3=e3: e.reduce_sum(ssum, e3, axis=AX.X),
                 deps=(aops[-1],), reads=("g_exps",), writes=("g_ssum",), tag="gc3")
            p.op("vector", lambda e, rec=rec, ssum=ssum: e.reciprocal(rec, ssum),
                 reads=("g_ssum",), writes=("g_rec",), tag="gc4")
            p.op("vector", lambda e, m1=m1, g3=g3: e.reduce_max(m1, g3, axis=AX.X),
                 reads=("g_gsb",), writes=("g_m1",), tag="gc5")
            for il in range(4):
                sl = slice(il * 8, il * 8 + 8)
                p.op("vector", lambda e, sl=sl, il=il, eq=eq, gsb=gsb, m1=m1: e.tensor_scalar(
                    eq[:, sl], gsb[:, sl], m1[:, il:il + 1], None, ALU.is_equal),
                    reads=("g_gsb", "g_m1"), writes=("g_eq",), tag="geq")
            for il in range(4):
                sl = slice(il * 8, il * 8 + 8)
                p.op("vector", lambda e, sl=sl, g2=g2, eq=eq, gsb=gsb: e.scalar_tensor_tensor(
                    g2[:, sl], eq[:, sl], -1e30, gsb[:, sl], ALU.mult, ALU.add),
                    reads=("g_eq", "g_gsb"), writes=("g_g2",), tag="gg2")
            g23 = g2.rearrange("p (t e) -> p t e", e=E)
            p.op("vector", lambda e, m2=m2, g23=g23: e.reduce_max(m2, g23, axis=AX.X),
                 reads=("g_g2",), writes=("g_m2",), tag="gc6")
            for il in range(4):
                sl = slice(il * 8, il * 8 + 8)
                p.op("vector", lambda e, sl=sl, il=il, mk2=mk2, gsb=gsb, m2=m2: e.tensor_scalar(
                    mk2[:, sl], gsb[:, sl], m2[:, il:il + 1], None, ALU.is_ge),
                    reads=("g_gsb", "g_m2"), writes=("g_mk",), tag="gmk")
            for il in range(4):
                sl = slice(il * 8, il * 8 + 8)
                p.op("vector", lambda e, sl=sl, il=il, exps=exps, rec=rec: e.tensor_scalar(
                    exps[:, sl], exps[:, sl], rec[:, il:il + 1], None, ALU.mult),
                    reads=("g_rec", "g_exps"), writes=("g_exps",), tag="gsm")
            cmb = p.op("vector", lambda e, gi=gi, exps=exps, mk2=mk2: e.tensor_mul(
                k.combine[:, gi * 32:(gi + 1) * 32], exps, mk2),
                reads=("g_exps", "g_mk"), writes=(f"comb{gi}",), tag="gcomb")
            trs = None
            for il in range(4):
                i = gi * 4 + il
                trs = p.op("tensor", lambda e, i=i, il=il: e.transpose(
                    k.PSB[0:8, il * 128:(il + 1) * 128],
                    k.combine[:, i * E:(i + 1) * E], k.identf[:]),
                    deps=(cmb, self.ld_identf), reads=(f"comb{gi}", "identf"),
                    writes=("psB0",), tag=f"trcmb{i}")
            p.op("vector", lambda e, gi=gi: e.tensor_copy(
                k.combineT[:, gi * 512:(gi + 1) * 512], k.PSB[0:8, 0:512]),
                deps=(trs,), reads=("psB0",), writes=(f"combT{gi}",), tag="evcmb")

        self.dbg_dump(f"combine{b}", k.combine[:], [128, NT * E], F32,
                      tuple(f"comb{gi}" for gi in range(NG)))

    EORD = (5, 4, 2, 3, 6, 7, 0, 1)

    def phase_C(self, b):
        k, p = self, self.p
        for i in range(NT):
            gi = i // 4
            par = i % 2
            mot = k.mo[:, par * D:(par + 1) * D]
            mkey = f"mo{par}"
            cb = None
            for h2 in range(2):
                cb = p.op("tensor", lambda e, i=i, h2=h2: e.matmul(
                    k.PSB[:, h2 * 512:(h2 + 1) * 512],
                    k.combineT[:, i * 128:(i + 1) * 128],
                    k.be[:, h2 * 512:(h2 + 1) * 512],
                    start=True, stop=True, skip_group_check=True),
                    reads=(f"combT{gi}", "be"),
                    writes=("psB0", "psB1") if h2 == 0 else (), tag=f"cb{i}")
            prev = p.op("vector", lambda e, mot=mot: e.tensor_copy(mot, k.PSB[:]),
                        deps=(cb,), reads=("psB0", "psB1"), writes=(mkey,), tag=f"evcb{i}")
            for pp, ei in enumerate(self.EORD):
                ps = (k.PSA, k.PSC)[pp % 2]
                pskeys = ("psA0", "psA1") if pp % 2 == 0 else ("psC0", "psC1")
                mm = None
                for c in range(NCH):
                    for h2 in range(2):
                        mm = p.op("tensor", lambda e, c=c, h2=h2, ei=ei, i=i, ps=ps: e.matmul(
                            ps[:, h2 * 512:(h2 + 1) * 512],
                            k.xnT[:, c * S + i * 128:c * S + (i + 1) * 128],
                            k.weW[ei][:, c * D + h2 * 512:c * D + (h2 + 1) * 512],
                            start=(c == 0), stop=(c == NCH - 1), skip_group_check=True),
                            reads=(f"xnT{i}", f"E{ei}"),
                            writes=pskeys if (c == 0 and h2 == 0) else (), tag=f"ex{i}_{ei}")
                prev = p.op("vector", lambda e, mot=mot, ps=ps, i=i, ei=ei: e.scalar_tensor_tensor(
                    mot, ps[:], k.combine[:, i * E + ei:i * E + ei + 1], mot, ALU.mult, ALU.add),
                    deps=(mm, prev), reads=pskeys + (f"comb{gi}", mkey),
                    writes=(mkey,), tag=f"stt{i}_{ei}")
            p.op("sync", lambda e, mot=mot, i=i: e.dma_start(
                out=k.moe_h[b, i * 128:(i + 1) * 128, :], in_=mot),
                reads=(mkey,), dma_sem="qS", tag=f"stmo{i}")

    def phase_B1a(self, b):
        k, p = self, self.p
        self.wmat_load(k.wk_h, k.wkW, writes=("wkW", "wkWb", "E5"), tag="wk")
        self.wmat_load(k.wv_h, k.wvW, writes=("wvW", "wvWb", "E4"), tag="wv")
        self.wmat_load(k.wm_h, k.wmW, writes=("wmW", "E6", "E7"), tag="wm")
        v_ones = k.vbuf[:].rearrange("p (s c) -> p s c", c=65)[:, :, 64:65]
        p.op("gpsimd", lambda e: e.memset(v_ones, 1.0),
             writes=("vones", "E2", "E3"), tag="vones")
        self.proj_kv(b, k.wkW, ("wkW", "wkWb"), k.wvW, ("wvW", "wvWb"),
                     k.bkT, "bkT", k.brow_v, "brow_v")

    def phase_B2a(self, b):
        k, p = self, self.p

        def slot_ops(names):
            dd = []
            for n in names:
                s = p.slots.get(n)
                if s:
                    if s["w"] is not None:
                        dd.append(s["w"])
                    dd.extend(s["r"])
            return tuple(dd)

        def half_load(src2d, dst, half, deps, writes, tag):
            src = src2d[half * 512:(half + 1) * 512, :].rearrange(
                "(c p) d -> p c d", p=128)
            d3 = dst[:, half * 4 * D:(half + 1) * 4 * D].rearrange(
                "p (c d) -> p c d", c=4)
            return p.op("gpsimd", lambda g: g.dma_start(out=d3, in_=src),
                        deps=deps, writes=writes, dma_sem="qG", tag=tag)

        half_load(k.wqkv_h[:, 1024:2048], k.wkW, 0,
                  slot_ops([f"pT{s}" for s in range(4)]), ("wkW",), "wk2a")
        half_load(k.wqkv_h[:, 1024:2048], k.wkW, 1,
                  slot_ops([f"a1n{il}" for il in range(4)]), ("wkWb",), "wk2b")
        half_load(k.wqkv_h[:, 2048:3072], k.wvW, 0,
                  slot_ops([f"qT{d}" for d in range(NCH)]), ("wvW",), "wv2a")
        half_load(k.wqkv_h[:, 2048:3072], k.wvW, 1,
                  slot_ops([f"a1nT{pq}_{c}" for pq in range(2) for c in range(NCH)]
                           + ["outT1"]), ("wvWb",), "wv2b")
        self.proj_kv(b, k.wkW, ("wkW", "wkWb"), k.wvW, ("wvW", "wvWb"),
                     k.bk2T, "bk2T", k.brow_v2, "brow_v2")

    def proj_kv(self, b, wkW, wk_keys, wvW, wv_keys, bkT, bk_key, brow_v, bv_key):
        k, p = self, self.p

        def kt_unit(u):
            d, tseg = u // NG, u % NG
            par = u % 2
            pskey = f"psA{par}"
            pcol = par * 512
            mm = None
            for c in range(NCH):
                mm = p.op("tensor", lambda e, c=c, d=d, tseg=tseg, pcol=pcol: e.matmul(
                    k.PSA[:, pcol:pcol + 512],
                    wkW[:, c * D + d * 128:c * D + (d + 1) * 128],
                    k.xnT[:, c * S + tseg * 512:c * S + (tseg + 1) * 512],
                    start=(c == 0), stop=(c == NCH - 1), skip_group_check=True),
                    reads=(wk_keys[c // 4],) + tuple(f"xnT{i}" for i in range(tseg * 4, tseg * 4 + 4)),
                    writes=(pskey,) if c == 0 else (), tag=f"kt{d}_{tseg}")
            p.op("vector", lambda e, d=d, tseg=tseg, pcol=pcol: e.tensor_scalar(
                k.kT[:, d * S + tseg * 512:d * S + (tseg + 1) * 512],
                k.PSA[:, pcol:pcol + 512], bkT[:, d:d + 1], None, ALU.add),
                deps=(mm,), reads=(pskey, bk_key), writes=("kT",), tag=f"evkt{d}_{tseg}")

        def v_unit(i):
            ps = (k.PSB, k.PSC)[i % 2]
            pskeys = ("psB0", "psB1") if i % 2 == 0 else ("psC0", "psC1")
            for c in range(NCH):
                for h2 in range(2):
                    p.op("tensor", lambda e, c=c, h2=h2, i=i, ps=ps: e.matmul(
                        ps[:, h2 * 512:(h2 + 1) * 512],
                        k.xnT[:, c * S + i * 128:c * S + (i + 1) * 128],
                        wvW[:, c * D + h2 * 512:c * D + (h2 + 1) * 512],
                        start=(c == 0), stop=False, skip_group_check=True),
                        reads=(wv_keys[c // 4], f"xnT{i}"),
                        writes=pskeys if (c == 0 and h2 == 0) else (), tag=f"v{i}")
            mm = None
            for h2 in range(2):
                mm = p.op("tensor", lambda e, h2=h2, ps=ps: e.matmul(
                    ps[:, h2 * 512:(h2 + 1) * 512], k.onesb[:],
                    brow_v[:, h2 * 512:(h2 + 1) * 512],
                    start=False, stop=True, skip_group_check=True),
                    reads=("onesb", bv_key), tag=f"vb{i}")
            dst = k.vbuf[:].rearrange("p (s c) -> p s c", c=65)[:, i * H:(i + 1) * H, 0:64]
            src = ps[:].rearrange("p (s c) -> p s c", c=64)
            p.op("vector", lambda e, dst=dst, src=src: e.tensor_copy(dst, src),
                 deps=(mm,), reads=pskeys, writes=("vbuf",), tag=f"evv{i}")

        for i in range(NT):
            kt_unit(2 * i)
            v_unit(i)
            kt_unit(2 * i + 1)

    def attn(self, b, causal):
        k, p = self, self.p
        wq_src = k.wq_h if causal else k.wqkv_h[:, 0:1024]
        bqT = k.bqT if causal else k.bq2T
        bq_key = "bqT" if causal else "bq2T"

        def emit_epilogue(g):
            for il in range(4):
                i = g * 4 + il
                par = il % 2
                if causal:
                    xop = p.op("sync", lambda e, i=i, par=par: e.dma_start(
                        out=k.xt[:, par * D:(par + 1) * D],
                        in_=k.x_h[b, i * 128:(i + 1) * 128, :]),
                        writes=(f"xt{par}",), dma_sem="qS", tag=f"rlx{i}")
                    for c in range(NCH):
                        pse = (k.PSE1, k.PSE2)[c % 2]
                        psekey = f"psE{c % 2 + 1}"
                        tt = p.op("tensor", lambda e, c=c, il=il, pse=pse: e.transpose(
                            pse[:, 512:640], k.a1n[:, il * D + c * 128:il * D + (c + 1) * 128],
                            k.identb[:]),
                            reads=(f"a1n{il}",), writes=(psekey,), tag=f"tb{i}_{c}")
                        p.op("vector", lambda e, c=c, par=par, pse=pse: e.tensor_copy(
                            k.a1nT[:, par * D + c * 128:par * D + (c + 1) * 128], pse[:, 512:640]),
                            deps=(tt,), reads=(psekey,), writes=(f"a1nT{par}_{c}",), tag=f"evtb{i}_{c}")
                    mm = None
                    for c in range(NCH):
                        for h2 in range(2):
                            mm = p.op("tensor", lambda e, c=c, h2=h2, par=par: e.matmul(
                                k.PSB[:, h2 * 512:(h2 + 1) * 512],
                                k.a1nT[:, par * D + c * 128:par * D + (c + 1) * 128],
                                k.wmW[:, c * D + h2 * 512:c * D + (h2 + 1) * 512],
                                start=(c == 0), stop=False, skip_group_check=True),
                                reads=(f"a1nT{par}_{c}", "wmW"),
                                writes=("psB0", "psB1") if (c == 0 and h2 == 0) else (),
                                tag=f"wm{i}")
                    for h2 in range(2):
                        mm = p.op("tensor", lambda e, h2=h2: e.matmul(
                            k.PSB[:, h2 * 512:(h2 + 1) * 512], k.onesb[:],
                            k.brow_m[:, h2 * 512:(h2 + 1) * 512],
                            start=False, stop=True, skip_group_check=True),
                            reads=("onesb", "brow_m"), tag=f"wmb{i}")
                    p.op("vector", lambda e, par=par: e.tensor_add(
                        k.acc[:, par * D:(par + 1) * D], k.PSB[:],
                        k.xt[:, par * D:(par + 1) * D]),
                        deps=(mm, xop), reads=("psB0", "psB1", f"xt{par}"),
                        writes=(f"acc{par}",), tag=f"pacc{i}")
                    p.op("sync", lambda e, i=i, par=par: e.dma_start(
                        out=k.partial_h[b, i * 128:(i + 1) * 128, :],
                        in_=k.acc[:, par * D:(par + 1) * D]),
                        reads=(f"acc{par}",), dma_sem="qS", tag=f"stp{i}")
                else:
                    pop = p.op("sync", lambda e, i=i, par=par: e.dma_start(
                        out=k.xt[:, par * D:(par + 1) * D],
                        in_=k.partial_h[b, i * 128:(i + 1) * 128, :]),
                        writes=(f"xt{par}",), dma_sem="qS", tag=f"ldp{i}")
                    mop = p.op("sync", lambda e, i=i, par=par: e.dma_start(
                        out=k.mo[:, par * D:(par + 1) * D],
                        in_=k.moe_h[b, i * 128:(i + 1) * 128, :]),
                        writes=(f"mo{par}",), dma_sem="qS", tag=f"ldmo{i}")
                    f1 = p.op("vector", lambda e, par=par, il=il: e.tensor_add(
                        k.acc[:, par * D:(par + 1) * D],
                        k.xt[:, par * D:(par + 1) * D],
                        k.a1n[:, il * D:(il + 1) * D]),
                        deps=(pop,), reads=(f"xt{par}", f"a1n{il}"),
                        writes=(f"acc{par}",), tag=f"fin1_{i}")
                    p.op("vector", lambda e, par=par: e.tensor_add(
                        k.acc[:, par * D:(par + 1) * D],
                        k.acc[:, par * D:(par + 1) * D],
                        k.mo[:, par * D:(par + 1) * D]),
                        deps=(mop, f1), reads=(f"mo{par}", f"acc{par}"),
                        writes=(f"acc{par}",), tag=f"fin2_{i}")
                    self.yst = p.op("sync", lambda e, i=i, par=par: e.dma_start(
                        out=k.y_h[b, i * 128:(i + 1) * 128, :],
                        in_=k.acc[:, par * D:(par + 1) * D]),
                        reads=(f"acc{par}",), dma_sem="qS", tag=f"sty{i}")

        pend_epi = None
        for g in range(NG):
            cmax = 4 * g + 3 if causal else NT - 1
            for d in range(NCH):
                par = d % 3
                ppar = d % 2
                pcol = ppar * 512
                pskey = f"psA{ppar}"
                wsl = k.wqW[:, par * 1024:(par + 1) * 1024]
                wkey = f"wq{par}"
                src = wq_src.rearrange("(c p) d -> p c d", p=128)[:, :, d * 128:(d + 1) * 128]
                d3 = wsl.rearrange("p (c t) -> p c t", c=NCH)
                wop = p.op("gpsimd", lambda e, d3=d3, src=src: e.dma_start(out=d3, in_=src),
                           writes=(wkey,), dma_sem="qG", tag=f"wqd{g}_{d}")
                mm = None
                for c in range(NCH):
                    mm = p.op("tensor", lambda e, c=c, g=g, wsl=wsl, pcol=pcol: e.matmul(
                        k.PSA[:, pcol:pcol + 512],
                        wsl[:, c * 128:(c + 1) * 128],
                        k.xnT[:, c * S + g * QG:c * S + (g + 1) * QG],
                        start=(c == 0), stop=(c == NCH - 1), skip_group_check=True),
                        deps=(wop,),
                        reads=(wkey,) + tuple(f"xnT{i}" for i in range(g * 4, g * 4 + 4)),
                        writes=(pskey,) if c == 0 else (), tag=f"qt{g}_{d}")
                p.op("vector", lambda e, d=d, g=g, bqT=bqT, pcol=pcol: e.tensor_scalar(
                    k.qTb[:, d * QG:(d + 1) * QG], k.PSA[:, pcol:pcol + 512],
                    bqT[:, d:d + 1], None, ALU.add),
                    deps=(mm,), reads=(pskey, bq_key), writes=(f"qT{d}",), tag=f"evqt{g}_{d}")
            if pend_epi is not None:
                emit_epilogue(pend_epi)
                pend_epi = None
            def emit_finish_il(pr, ob, obkey, il):
                hA, hB = 2 * pr, 2 * pr + 1
                if True:
                    par = (pr * 4 + il) % 2
                    pse = (k.PSE1, k.PSE2)[par]
                    psekey = f"psE{par + 1}"
                    p.op("tensor", lambda e, il=il, pse=pse, ob=ob: e.transpose(
                        pse[:, 0:65], ob[0:65, il * 128:(il + 1) * 128], k.identb[0:65, 0:65]),
                        reads=(obkey,), writes=(psekey,), tag=f"t1{g}_{pr}_{il}")
                    t2 = p.op("tensor", lambda e, il=il, pse=pse, ob=ob: e.transpose(
                        pse[:, 128:193], ob[0:65, 512 + il * 128:512 + (il + 1) * 128],
                        k.identb[0:65, 0:65]),
                        reads=(obkey,), writes=(psekey,), tag=f"t2{g}_{pr}_{il}")
                    recA = k.small[:, 8 + par * 2:9 + par * 2]
                    recB = k.small[:, 9 + par * 2:10 + par * 2]
                    r1 = p.op("vector", lambda e, pse=pse, recA=recA: e.reciprocal(recA, pse[:, 64:65]),
                              deps=(t2,), reads=(psekey,), writes=(f"recA{par}",), tag="rcA")
                    r2 = p.op("vector", lambda e, pse=pse, recB=recB: e.reciprocal(recB, pse[:, 192:193]),
                              reads=(psekey,), writes=(f"recB{par}",), tag="rcB")
                    dstA = k.a1n[:, il * D + hA * 64: il * D + hA * 64 + 64]
                    dstB = k.a1n[:, il * D + hB * 64: il * D + hB * 64 + 64]
                    p.op("vector", lambda e, pse=pse, dstA=dstA, recA=recA: e.tensor_scalar(
                        dstA, pse[:, 0:64], recA[:, 0:1], None, ALU.mult),
                        deps=(r1,), reads=(psekey, f"recA{par}"),
                        writes=(f"a1n{il}",), tag="nrmA")
                    p.op("vector", lambda e, pse=pse, dstB=dstB, recB=recB: e.tensor_scalar(
                        dstB, pse[:, 128:192], recB[:, 0:1], None, ALU.mult),
                        deps=(r2,), reads=(psekey, f"recB{par}"),
                        writes=(f"a1n{il}",), tag="nrmB")
            def finish_steps(pr, ob, obkey):
                for il in range(4):
                    emit_finish_il(pr, ob, obkey, il)
                    yield

            def pair_steps(pr):
                hA, hB = 2 * pr, 2 * pr + 1
                av_last = None
                pend = []
                def emit_av(dep_probs, c, slot):
                    nonlocal av_last
                    p.op("tensor", lambda e, c=c, hA=hA, slot=slot, cmax=cmax: e.matmul(
                        k.PSC[0:65, 0:512],
                        k.vbuf[:, (c * H + hA) * 65:(c * H + hA) * 65 + 65],
                        k.probsT[:, slot * 1024:slot * 1024 + 512],
                        start=(c == 0), stop=(c == cmax), skip_group_check=True),
                        deps=(dep_probs,), reads=("vbuf", "vones", f"pT{slot}"),
                        writes=("psC0",) if c == 0 else (), tag=f"avA{g}_{pr}_{c}")
                    av_last = p.op("tensor", lambda e, c=c, hB=hB, slot=slot, cmax=cmax: e.matmul(
                        k.PSC[0:65, 512:1024],
                        k.vbuf[:, (c * H + hB) * 65:(c * H + hB) * 65 + 65],
                        k.probsT[:, slot * 1024 + 512:slot * 1024 + 1024],
                        start=(c == 0), stop=(c == cmax), skip_group_check=True),
                        deps=(dep_probs,), reads=("vbuf", "vones", f"pT{slot}"),
                        writes=("psC1",) if c == 0 else (), tag=f"avB{g}_{pr}_{c}")
                for c in range(cmax + 1):
                    yield
                    sps = (k.PSA, k.PSB)[c % 2]
                    skeys = ("psA0", "psA1") if c % 2 == 0 else ("psB0", "psB1")
                    slot = c % 4
                    p.op("tensor", lambda e, c=c, pr=pr, sps=sps: e.matmul(
                        sps[:, 0:512],
                        k.kT[0:64, pr * S + c * 128:pr * S + (c + 1) * 128],
                        k.qTb[0:64, pr * QG:(pr + 1) * QG],
                        start=True, stop=True, skip_group_check=True,
                        tile_position=(0, 0)),
                        reads=("kT", f"qT{pr}"), writes=(skeys[0],), tag=f"scA{g}_{pr}_{c}")
                    sB = p.op("tensor", lambda e, c=c, pr=pr, sps=sps: e.matmul(
                        sps[:, 512:1024],
                        k.kT[64:128, pr * S + c * 128:pr * S + (c + 1) * 128],
                        k.qTb[64:128, pr * QG:(pr + 1) * QG],
                        start=True, stop=True, skip_group_check=True,
                        tile_position=(64, 0)),
                        reads=("kT", f"qT{pr}"), writes=(skeys[1],), tag=f"scB{g}_{pr}_{c}")
                    ex = p.op("scalar", lambda e, slot=slot, sps=sps: e.activation(
                        k.probsT[:, slot * 1024:(slot + 1) * 1024], sps[:],
                        AF.Exp, bias=-CSH, scale=0.125),
                        deps=(sB,), reads=skeys + ("c_csh",), writes=(f"pT{slot}",), tag=f"exp{g}_{pr}_{c}")
                    dep_probs = ex
                    if causal and c >= 4 * g:
                        j = c - 4 * g
                        msl = k.masks[:, j * 512:(j + 1) * 512]
                        for hh in range(2):
                            sl = k.probsT[:, slot * 1024 + hh * 512: slot * 1024 + hh * 512 + 512]
                            dep_probs = p.op("vector", lambda e, sl=sl, msl=msl: e.tensor_mul(
                                sl, sl, msl),
                                deps=(dep_probs,), reads=(f"pT{slot}", "masks"),
                                writes=(f"pT{slot}",), tag=f"msk{g}_{pr}_{c}")
                    pend.append((dep_probs, c, slot))
                    if c % 2 == 0:
                        while len(pend) > 1:
                            emit_av(*pend.pop(0))
                while pend:
                    emit_av(*pend.pop(0))
                ob = (k.outT, k.outT2)[pr % 2]
                obkey = f"outT{pr % 2}"
                p.op("vector", lambda e, ob=ob: e.tensor_copy(ob[0:65, :], k.PSC[0:65, :]),
                     deps=(av_last,), reads=("psC0", "psC1"),
                     writes=(obkey,), tag=f"evout{g}_{pr}")

            fin_gen = None
            for pr in range(NPAIR):
                pg = pair_steps(pr)
                for _ in pg:
                    if fin_gen is not None:
                        next(fin_gen, None)
                while fin_gen is not None and next(fin_gen, StopIteration) is not StopIteration:
                    pass
                fin_gen = finish_steps(pr, (k.outT, k.outT2)[pr % 2], f"outT{pr % 2}")
            while next(fin_gen, StopIteration) is not StopIteration:
                pass
            pend_epi = g
        emit_epilogue(pend_epi)

    def phase_fin(self):
        p = self.p
        sync_lasts = [o for s, o in p.lane_last.items() if s.startswith("qS")]
        gp_lasts = [o for s, o in p.lane_last.items() if s.startswith("qG")]
        p.op("sync", lambda e: e.nop(), deps=[self.yst] + sync_lasts, tag="finS")
        p.op("gpsimd", lambda e: e.nop(), deps=gp_lasts, tag="finG")


def build_program(dbg=()):
    kb = KB(dbg)
    kb.build()
    return kb


def _make_masks():
    k = np.arange(128)[:, None]
    q = np.arange(512)[None, :]
    cols = [((q - k) >= 128 * j).astype(np.float32) for j in range(4)]
    return np.concatenate(cols, axis=1).astype(ml_dtypes.bfloat16)


def make_in_maps(inputs):
    x = np.asarray(inputs["L0_attn_x"])
    common = {
        "wq": np.asarray(inputs["L0_attn_Wq"]), "bq": np.asarray(inputs["L0_attn_bq"]),
        "wk": np.asarray(inputs["L0_attn_Wk"]), "bk": np.asarray(inputs["L0_attn_bk"]),
        "wv": np.asarray(inputs["L0_attn_Wv"]), "bv": np.asarray(inputs["L0_attn_bv"]),
        "wm": np.asarray(inputs["L0_attn_Wm"]), "bm": np.asarray(inputs["L0_attn_bm"]),
        "wqkv": np.asarray(inputs["L0_ffn_W_qkv"]), "bqkv": np.asarray(inputs["L0_ffn_b_qkv"]),
        "wg": np.asarray(inputs["L0_ffn_W_gate"]), "bg": np.asarray(inputs["L0_ffn_b_gate"]),
        "we": np.asarray(inputs["L0_ffn_W_experts"]), "be": np.asarray(inputs["L0_ffn_b_experts"]),
        "identb": np.eye(128, dtype=np.float32).astype(ml_dtypes.bfloat16),
        "identf": np.eye(128, dtype=np.float32),
        "onesb": np.ones((1, 128), dtype=np.float32).astype(ml_dtypes.bfloat16),
        "masks": _make_masks(),
    }
    in_maps = []
    for core in range(8):
        m = dict(common)
        m["x"] = np.ascontiguousarray(x[core * BPC:(core + 1) * BPC])
        in_maps.append(m)
    return in_maps


def kernel(**inputs):
    kb = build_program()
    in_maps = make_in_maps(inputs)
    res = run_bass_kernel_spmd(kb.nc, in_maps, core_ids=list(range(8)))
    return np.concatenate([r["y"] for r in res.results], axis=0)


# version 36
# speedup vs baseline: 1.2062x; 1.0873x over previous
import numpy as np
import ml_dtypes
from contextlib import ExitStack

import concourse.bass as bass
import concourse.mybir as mybir
from concourse.bass_utils import run_bass_kernel_spmd

F32 = mybir.dt.float32
BF16 = mybir.dt.bfloat16
AF = mybir.ActivationFunctionType
ALU = mybir.AluOpType
AX = mybir.AxisListType

BPC = 2
S = 2048
D = 1024
H = 16
E = 8
NT = S // 128
NCH = D // 128
NG = 4
QG = 512
NPAIR = 8
CSH = 20.0
EPS = 1e-5

ENGS = ("tensor", "vector", "scalar", "gpsimd", "sync")


class Op:
    __slots__ = ("eng", "fn", "deps", "sem", "inc_n", "tok", "needed", "tag", "is_dma")

    def __init__(self, eng, fn, sem, inc_n, tag, is_dma):
        self.eng = eng
        self.fn = fn
        self.deps = []
        self.sem = sem
        self.inc_n = inc_n
        self.tok = None
        self.needed = False
        self.tag = tag
        self.is_dma = is_dma


NLANES = 8


class Prog:
    def __init__(self):
        self.ops = {e: [] for e in ENGS}
        self.all = []
        self.slots = {}
        self.eng_sem = {"tensor": "mP", "vector": "mV", "scalar": "mA",
                        "gpsimd": "mG", "sync": "mS"}
        self.qcnt = {"qS": 0, "qG": 0}
        self.lane_last = {}

    def _add_dep(self, o, d, raw):
        if d is None or d is o:
            return
        same = d.eng == o.eng and not (d.is_dma or o.is_dma)
        if same and (not raw or o.eng == "tensor"):
            return
        o.deps.append(d)

    def op(self, eng, fn, deps=(), reads=(), writes=(), dma_sem=None, tag=""):
        is_dma = dma_sem is not None
        if is_dma:
            n = self.qcnt[dma_sem]
            self.qcnt[dma_sem] = n + 1
            lane = n % NLANES
            sem = f"{dma_sem}{lane}"
            o = Op(eng, fn, sem, 16, tag, True)
            o.tok = (sem, 16 * (n // NLANES + 1))
            prev = self.lane_last.get(sem)
            if prev is not None:
                o.deps.append(prev)
            self.lane_last[sem] = o
        else:
            o = Op(eng, fn, self.eng_sem[eng], 1, tag, False)
        for d in deps:
            self._add_dep(o, d, raw=True)
        for r in reads:
            s = self.slots.get(r)
            if s and s["w"] is not None:
                self._add_dep(o, s["w"], raw=True)
        for w in writes:
            s = self.slots.setdefault(w, {"w": None, "r": []})
            self._add_dep(o, s["w"], raw=False)
            for r in s["r"]:
                self._add_dep(o, r, raw=False)
        for r in reads:
            self.slots.setdefault(r, {"w": None, "r": []})["r"].append(o)
        for w in writes:
            s = self.slots[w]
            s["w"] = o
            s["r"] = []
        self.ops[eng].append(o)
        self.all.append(o)
        return o

    def barrier(self):
        return [self.ops[e][-1] for e in ENGS if self.ops[e]]

    def emit(self, nc, sems):
        seq = {id(o): i for i, o in enumerate(self.all)}
        for o in self.all:
            if o.is_dma:
                o.needed = True
            best = {}
            dmas, seen = [], set()
            for d in o.deps:
                if id(d) in seen:
                    continue
                seen.add(id(d))
                if d.is_dma:
                    dmas.append(d)
                elif d.eng not in best or seq[id(best[d.eng])] < seq[id(d)]:
                    best[d.eng] = d
            dd = dmas + list(best.values())
            for d in dd:
                d.needed = True
            o.deps = dd
        cnt = {}
        for o in self.all:
            if o.needed and o.tok is None:
                cnt[o.sem] = cnt.get(o.sem, 0) + o.inc_n
                o.tok = (o.sem, cnt[o.sem])
        self.final_counts = cnt
        stats = {e: [0, 0] for e in ENGS}

        def emit_engine(engobj, ename):
            emitted = {}
            for o in self.ops[ename]:
                for d in o.deps:
                    if d.tok is None:
                        continue
                    sname, v = d.tok
                    if emitted.get(sname, 0) >= v:
                        continue
                    engobj.wait_ge(sems[sname], v)
                    emitted[sname] = v
                    stats[ename][1] += 1
                inst = o.fn(engobj)
                if o.needed:
                    inst.then_inc(sems[o.sem], o.inc_n)
                stats[ename][0] += 1

        with nc.Block("main") as blk:
            blk.tensor(lambda e: emit_engine(e, "tensor"))
            blk.vector(lambda e: emit_engine(e, "vector"))
            blk.scalar(lambda e: emit_engine(e, "scalar"))
            blk.gpsimd(lambda e: emit_engine(e, "gpsimd"))
            blk.sync(lambda e: emit_engine(e, "sync"))
        self.stats = stats


ALIGN = 32


def _al(x):
    return (x + ALIGN - 1) // ALIGN * ALIGN


class KB:
    def __init__(self, dbg=()):
        self.dbg = set(dbg)
        self.nc = bass.Bass("TRN2", target_bir_lowering=False)
        self.stack = ExitStack()
        self.p = Prog()
        self.sems = {}
        self.dbg_names = []

    def sb(self, name, shape, dtype, at_off):
        nc = self.nc
        name = "s_" + name
        h = nc.virtual_sbuf_tensor(name, shape, dtype)
        nc._tensor(h.name, shape, dtype, type="SB")
        mloc = nc.lookup_mloc(h)
        mloc.allocated = True
        mloc.addr = self.arena_base + at_off
        return h

    def build(self):
        nc, st = self.nc, self.stack
        ap = lambda t: t.ap()
        dt = nc.dram_tensor
        self.x_h = ap(dt("x", [BPC, S, D], F32, kind="ExternalInput"))
        self.wq_h = ap(dt("wq", [D, D], F32, kind="ExternalInput"))
        self.bq_h = ap(dt("bq", [D], F32, kind="ExternalInput"))
        self.wk_h = ap(dt("wk", [D, D], F32, kind="ExternalInput"))
        self.bk_h = ap(dt("bk", [D], F32, kind="ExternalInput"))
        self.wv_h = ap(dt("wv", [D, D], F32, kind="ExternalInput"))
        self.bv_h = ap(dt("bv", [D], F32, kind="ExternalInput"))
        self.wm_h = ap(dt("wm", [D, D], F32, kind="ExternalInput"))
        self.bm_h = ap(dt("bm", [D], F32, kind="ExternalInput"))
        self.wqkv_h = ap(dt("wqkv", [D, 3 * D], F32, kind="ExternalInput"))
        self.bqkv_h = ap(dt("bqkv", [3 * D], F32, kind="ExternalInput"))
        self.wg_h = ap(dt("wg", [D, E], F32, kind="ExternalInput"))
        self.bg_h = ap(dt("bg", [E], F32, kind="ExternalInput"))
        self.we_h = ap(dt("we", [E, D, D], F32, kind="ExternalInput"))
        self.be_h = ap(dt("be", [E, D], F32, kind="ExternalInput"))
        self.identb_h = ap(dt("identb", [128, 128], BF16, kind="ExternalInput"))
        self.identf_h = ap(dt("identf", [128, 128], F32, kind="ExternalInput"))
        self.onesb_h = ap(dt("onesb", [1, 128], BF16, kind="ExternalInput"))
        self.masks_h = ap(dt("masks", [128, 4 * 512], BF16, kind="ExternalInput"))
        self.y_h = ap(dt("y", [BPC, S, D], F32, kind="ExternalOutput"))
        pk = "ExternalOutput" if "partial" in self.dbg else "Internal"
        mk = "ExternalOutput" if "moe" in self.dbg else "Internal"
        self.partial_h = ap(dt("partial", [BPC, S, D], F32, kind=pk))
        self.moe_h = ap(dt("moe", [BPC, S, D], F32, kind=mk))

        sem_names = ["mP", "mV", "mA", "mG", "mS"]
        sem_names += [f"qS{i}" for i in range(8)] + [f"qG{i}" for i in range(8)]
        for name in sem_names:
            self.sems[name] = st.enter_context(nc.semaphore(name))

        limit = nc.sbuf_bytes_remaining - 512
        arena = st.enter_context(nc.sbuf_tensor("arena", [128, limit], mybir.dt.uint8))
        self.arena_base = nc.lookup_mloc(arena).addr
        off = [0]

        def place(n):
            a = off[0]
            off[0] = _al(a + n)
            return a

        o_identb = place(256); o_identf = place(512); o_onesb = place(256)
        o_brow_v = place(2048); o_brow_m = place(2048); o_brow_v2 = place(2048)
        o_brow_g = place(64); o_be = place(2048); o_bstage = place(4096)
        o_bqT = place(32); o_bkT = place(32); o_bq2T = place(32); o_bk2T = place(32)
        o_xnT = place(NCH * S * 2)
        o_combine = place(NT * E * 4)
        o_combineT = place(NT * 128 * 2)
        o_xt = place(2 * D * 4)
        o_mo = place(2 * D * 4)
        o_xn = place(2 * D * 2)
        o_small = place(4096)
        o_const = place(64)
        o_wg = place(NCH * E * 2)
        o_masks = place(4 * 512 * 2)
        o_Q = place(131584)
        assert off[0] <= limit, (off[0], limit)
        self.sbuf_used = off[0]

        q = o_Q
        o_kT = q
        o_vbuf = q + 32768
        o_qT = q + 66048
        o_spare = q + 74240
        o_probsT = q + 82432
        o_a1n = q + 90624
        o_outT = q + 98816
        o_acc = q + 100864
        o_wm = q + 109056
        o_wq = q + 125440
        o_we = [q, q + 16384, q + 32768, q + 49152,
                q + 66048, q + 82432, q + 98816, q + 115200]

        sb = self.sb
        self.identb = sb("identb", [128, 128], BF16, o_identb)
        self.identf = sb("identf", [128, 128], F32, o_identf)
        self.onesb = sb("onesb", [1, 128], BF16, o_onesb)
        self.brow_v = sb("brow_v", [1, D], BF16, o_brow_v)
        self.brow_m = sb("brow_m", [1, D], BF16, o_brow_m)
        self.brow_v2 = sb("brow_v2", [1, D], BF16, o_brow_v2)
        self.brow_g = sb("brow_g", [1, E], BF16, o_brow_g)
        self.be = sb("be", [E, D], BF16, o_be)
        self.bstage = sb("bstage", [8, 128], F32, o_bstage)
        self.bqT = sb("bqT", [128, 8], F32, o_bqT)
        self.bkT = sb("bkT", [128, 8], F32, o_bkT)
        self.bq2T = sb("bq2T", [128, 8], F32, o_bq2T)
        self.bk2T = sb("bk2T", [128, 8], F32, o_bk2T)
        self.xnT = sb("xnT", [128, NCH * S], BF16, o_xnT)
        self.combine = sb("combine", [128, NT * E], F32, o_combine)
        self.combineT = sb("combineT", [E, NT * 128], BF16, o_combineT)
        self.xt = sb("xt", [128, 2 * D], F32, o_xt)
        self.mo = sb("mo", [128, 2 * D], F32, o_mo)
        self.xn = sb("xn", [128, 2 * D], BF16, o_xn)
        self.small = sb("small", [128, 1024], F32, o_small)
        self.consts = sb("consts", [128, 2], F32, o_const)
        self.kT = sb("kT", [128, NCH * S], BF16, o_kT)
        self.vbuf = sb("vbuf", [128, NT * H * 65], BF16, o_vbuf)
        self.qTb = sb("qTb", [128, NCH * QG], BF16, o_qT)
        self.a1nT = sb("a1nT", [128, 2 * D], BF16, o_spare)
        self.wgW = sb("wgW", [128, NCH * E], BF16, o_wg)
        self.masks = sb("masks", [128, 4 * 512], BF16, o_masks)
        self.probsT = sb("probsT", [128, 4 * 1024], BF16, o_probsT)
        self.a1n = sb("a1n", [128, 4 * D], BF16, o_a1n)
        self.outT = sb("outT", [65, 2 * QG], BF16, o_outT)
        self.outT2 = sb("outT2", [65, 2 * QG], BF16, o_spare + 4096)
        self.acc = sb("acc", [128, 2 * D], F32, o_acc)
        self.wmW = sb("wmW", [128, NCH * D], BF16, o_wm)
        self.wqW = sb("wqW", [128, 3 * 1024], BF16, o_wq)
        self.weW = [sb(f"weW{e}", [128, NCH * D], BF16, o_we[e]) for e in range(E)]
        self.wkW = sb("wkW", [128, NCH * D], BF16, o_probsT)
        self.wvW = sb("wvW", [128, NCH * D], BF16, o_qT)

        self.PSA = st.enter_context(nc.psum_tensor("PSA", [128, 1024], F32))
        self.PSB = st.enter_context(nc.psum_tensor("PSB", [128, 1024], F32))
        self.PSC = st.enter_context(nc.psum_tensor("PSC", [128, 1024], F32))
        self.PSE1 = st.enter_context(nc.psum_tensor("PSE1", [128, 1024], BF16))
        self.PSE2 = st.enter_context(nc.psum_tensor("PSE2", [128, 1024], BF16))

        nc.const_aps.register(F32, EPS, self.consts[:, 0:1])
        nc.const_aps.register(F32, -CSH, self.consts[:, 1:2])
        self.phase_w0()
        for b in range(BPC):
            self.phase_A(b)
            self.phase_C(b)
            self.phase_B1a(b)
            self.attn(b, causal=True)
            self.phase_B2a(b)
            self.attn(b, causal=False)
        self.phase_fin()
        self.p.emit(self.nc, self.sems)
        return self

    def wmat_load(self, src2d, dst, deps=(), writes=(), tag="w"):
        src = src2d.rearrange("(c p) d -> p c d", p=128)
        d3 = dst[:].rearrange("p (c d) -> p c d", c=NCH)
        return self.p.op("gpsimd", lambda g: g.dma_start(out=d3, in_=src),
                         deps=deps, writes=writes, dma_sem="qG", tag=tag)

    def dbg_dump(self, key, src_ap, shape, dtype, dep_reads):
        if key not in self.dbg:
            return
        name = "dbg_" + key
        h = self.nc.dram_tensor(name, shape, dtype, kind="ExternalOutput").ap()
        self.dbg_names.append(name)
        self.p.op("sync", lambda s: s.dma_start(out=h, in_=src_ap),
                  reads=dep_reads, dma_sem="qS", tag="dbg" + key)

    def phase_w0(self):
        k, p = self, self.p
        g = lambda fn, **kw: p.op("gpsimd", fn, dma_sem="qG", **kw)
        self.ld_identb = g(lambda e: e.dma_start(out=k.identb[:], in_=k.identb_h), writes=("identb",))
        self.ld_identf = g(lambda e: e.dma_start(out=k.identf[:], in_=k.identf_h), writes=("identf",))
        g(lambda e: e.dma_start(out=k.onesb[:], in_=k.onesb_h), writes=("onesb",))
        g(lambda e: e.dma_start(out=k.brow_v[:], in_=k.bv_h[None, :]), writes=("brow_v",))
        g(lambda e: e.dma_start(out=k.brow_m[:], in_=k.bm_h[None, :]), writes=("brow_m",))
        g(lambda e: e.dma_start(out=k.brow_v2[:], in_=k.bqkv_h[None, 2048:3072]), writes=("brow_v2",))
        g(lambda e: e.dma_start(out=k.brow_g[:], in_=k.bg_h[None, :]), writes=("brow_g",))
        g(lambda e: e.dma_start(out=k.be[:], in_=k.be_h), writes=("be",))
        self.wmat_load(k.wg_h, k.wgW, writes=("wgW",), tag="wg")
        g(lambda e: e.dma_start(out=k.masks[:], in_=k.masks_h), writes=("masks",))
        p.op("gpsimd", lambda e: e.memset(k.consts[:, 0:1], EPS), writes=("c_eps",), tag="ceps")
        p.op("gpsimd", lambda e: e.memset(k.consts[:, 1:2], -CSH), writes=("c_csh",), tag="ccsh")
        for nm, src_ap, dstT, wkey in (
                ("bq", k.bq_h[None, :], k.bqT, "bqT"),
                ("bk", k.bk_h[None, :], k.bkT, "bkT"),
                ("bq2", k.bqkv_h[None, 0:1024], k.bq2T, "bq2T"),
                ("bk2", k.bqkv_h[None, 1024:2048], k.bk2T, "bk2T")):
            src8 = src_ap.rearrange("o (c p) -> (o c) p", p=128)
            p.op("sync", lambda e, s=src8: e.dma_start(out=k.bstage[:], in_=s),
                 writes=("bstage",), dma_sem="qS", tag="b" + nm)
            last = p.op("tensor",
                        lambda e: e.transpose(k.PSA[:, 0:8], k.bstage[:],
                                              k.identf[0:8, 0:8]),
                        deps=(self.ld_identf,), reads=("bstage",),
                        writes=("psA0",), tag="trb" + nm)
            p.op("vector", lambda e, d=dstT: e.tensor_copy(d[:], k.PSA[:, 0:8]),
                 deps=(last,), reads=("psA0",), writes=(wkey,), tag="evb" + nm)

    def phase_A(self, b):
        k, p = self, self.p
        sm = k.small

        bar = tuple(p.barrier()) if b > 0 else ()
        qs_lasts = tuple(o for s, o in p.lane_last.items() if s.startswith("qS"))
        self.we_ops = []
        for e in (5, 4, 2, 3, 7, 0, 1, 6):
            o = self.wmat_load(k.we_h[e], k.weW[e],
                               deps=bar + qs_lasts if e == 6 else bar,
                               writes=(f"E{e}",), tag=f"we{e}")
            self.we_ops.append(o)

        def ln_steps(i):
            par = i % 2
            xt = k.xt[:, par * D:(par + 1) * D]
            xnb = k.xn[:, par * D:(par + 1) * D]
            xkey, xnkey = f"xt{par}", f"xn{par}"
            base = 16 * par
            s1, s2 = sm[:, base:base + 1], sm[:, base + 1:base + 2]
            mean, msq = sm[:, base + 2:base + 3], sm[:, base + 3:base + 4]
            var, lnv = sm[:, base + 4:base + 5], sm[:, base + 5:base + 6]
            rstd = sm[:, base + 6:base + 7]
            sq = k.mo[:, par * D:(par + 1) * D]
            sqkey = f"mo{par}"
            sx = f"_{par}"
            yield lambda: p.op("sync", lambda e: e.dma_start(
                out=xt, in_=k.x_h[b, i * 128:(i + 1) * 128, :]),
                writes=(xkey,), dma_sem="qS", tag=f"ldx{i}")
            yield lambda: p.op("scalar", lambda e: e.activation(
                xnb, xt, AF.Copy, bias=0.0, scale=1.0, accum_out=s1),
                reads=(xkey,), writes=(xnkey, "ln_s1" + sx), tag="ln1")
            yield lambda: p.op("scalar", lambda e: e.activation(
                sq, xt, AF.Square, bias=0.0, scale=1.0, accum_out=s2),
                reads=(xkey,), writes=(sqkey, "ln_s2" + sx), tag="ln3")
            yield lambda: p.op("vector", lambda e: e.tensor_scalar_mul(mean, s1, 1.0 / D),
                               reads=("ln_s1" + sx,), writes=("ln_mean" + sx,), tag="ln4")
            yield lambda: p.op("vector", lambda e: e.tensor_mul(msq, mean, mean),
                               reads=("ln_mean" + sx,), writes=("ln_msq" + sx,), tag="ln5")
            yield lambda: p.op("vector", lambda e: e.tensor_scalar_mul(var, s2, 1.0 / D),
                               reads=("ln_s2" + sx,), writes=("ln_var" + sx,), tag="ln6")
            yield lambda: p.op("vector", lambda e: e.tensor_sub(var, var, msq),
                               reads=("ln_msq" + sx, "ln_var" + sx), writes=("ln_var" + sx,), tag="ln7")
            yield lambda: p.op("scalar", lambda e: e.activation(lnv, var, AF.Ln, bias=EPS, scale=1.0),
                               reads=("ln_var" + sx, "c_eps"), writes=("ln_lnv" + sx,), tag="ln8")
            yield lambda: p.op("scalar", lambda e: e.activation(rstd, lnv, AF.Exp, bias=0.0, scale=-0.5),
                               reads=("ln_lnv" + sx,), writes=("ln_rstd" + sx,), tag="ln9")
            yield lambda: p.op("vector", lambda e: e.tensor_scalar(
                xnb, xt, mean[:, 0:1], rstd[:, 0:1], ALU.subtract, ALU.mult),
                reads=(xkey, "ln_mean" + sx, "ln_rstd" + sx), writes=(xnkey,), tag="ln12")
            pse = (k.PSE1, k.PSE2)[par]
            psekey = f"psE{par + 1}"
            def trs():
                last = None
                for c in range(NCH):
                    last = p.op("tensor", lambda e, c=c: e.transpose(
                        pse[:, c * 128:(c + 1) * 128], xnb[:, c * 128:(c + 1) * 128], k.identb[:]),
                        deps=(self.ld_identb,), reads=(xnkey,), writes=(psekey,), tag=f"trxn{i}")
                return last
            yield trs
            dst = k.xnT[:].rearrange("p (c t) -> p c t", c=NCH)[:, :, i * 128:(i + 1) * 128]
            src = pse[:].rearrange("p (c t) -> p c t", c=NCH)
            yield lambda: p.op("vector", lambda e: e.tensor_copy(dst, src),
                               reads=(psekey,), writes=(f"xnT{i}",), tag=f"evxn{i}")

        for i0 in range(0, NT, 2):
            gens = [ln_steps(i0), ln_steps(i0 + 1)]
            done = [False, False]
            while not all(done):
                for gi in range(2):
                    if done[gi]:
                        continue
                    try:
                        next(gens[gi])()
                    except StopIteration:
                        done[gi] = True

        self.dbg_dump(f"xnT{b}", k.xnT[:], [128, NCH * S], BF16,
                      tuple(f"xnT{i}" for i in range(NT)))

        for gi in range(NG):
            gsb = sm[:, 32:64]
            exps = sm[:, 64:96]
            nm0 = sm[:, 96:100]
            ssum, rec = sm[:, 100:104], sm[:, 104:108]
            m1, m2 = sm[:, 108:112], sm[:, 112:116]
            eq, g2, mk2 = sm[:, 116:148], sm[:, 148:180], sm[:, 180:212]
            mmop = None
            for il in range(4):
                i = gi * 4 + il
                col = il * 8
                for c in range(NCH):
                    p.op("tensor", lambda e, c=c, i=i, col=col: e.matmul(
                        k.PSA[:, col:col + 8],
                        k.xnT[:, c * S + i * 128:c * S + (i + 1) * 128],
                        k.wgW[:, c * E:(c + 1) * E],
                        start=(c == 0), stop=False, skip_group_check=True),
                        reads=(f"xnT{i}", "wgW"),
                        writes=("psA0",) if (c == 0 and il == 0) else (), tag=f"gmm{i}")
                mmop = p.op("tensor", lambda e, col=col: e.matmul(
                    k.PSA[:, col:col + 8], k.onesb[:], k.brow_g[:],
                    start=False, stop=True, skip_group_check=True),
                    reads=("onesb", "brow_g"), tag=f"gbias{i}")
            p.op("vector", lambda e, gsb=gsb: e.tensor_copy(gsb, k.PSA[:, 0:32]),
                 deps=(mmop,), reads=("psA0",), writes=("g_gsb",), tag="gc1")
            g3 = gsb.rearrange("p (t e) -> p t e", e=E)
            p.op("vector", lambda e, nm0=nm0, g3=g3: e.reduce_max(nm0, g3, axis=AX.X, negate=True),
                 reads=("g_gsb",), writes=("g_nm0",), tag="gc2")
            aops = []
            for il in range(4):
                sl = slice(il * 8, il * 8 + 8)
                aops.append(p.op("scalar", lambda e, sl=sl, il=il, exps=exps, gsb=gsb, nm0=nm0: e.activation(
                    exps[:, sl], gsb[:, sl], AF.Exp, bias=nm0[:, il:il + 1], scale=1.0),
                    reads=("g_gsb", "g_nm0"), writes=("g_exps",), tag="gexp"))
            e3 = exps.rearrange("p (t e) -> p t e", e=E)
            p.op("vector", lambda e, ssum=ssum, e3=e3: e.reduce_sum(ssum, e3, axis=AX.X),
                 deps=(aops[-1],), reads=("g_exps",), writes=("g_ssum",), tag="gc3")
            p.op("vector", lambda e, rec=rec, ssum=ssum: e.reciprocal(rec, ssum),
                 reads=("g_ssum",), writes=("g_rec",), tag="gc4")
            p.op("vector", lambda e, m1=m1, g3=g3: e.reduce_max(m1, g3, axis=AX.X),
                 reads=("g_gsb",), writes=("g_m1",), tag="gc5")
            for il in range(4):
                sl = slice(il * 8, il * 8 + 8)
                p.op("vector", lambda e, sl=sl, il=il, eq=eq, gsb=gsb, m1=m1: e.tensor_scalar(
                    eq[:, sl], gsb[:, sl], m1[:, il:il + 1], None, ALU.is_equal),
                    reads=("g_gsb", "g_m1"), writes=("g_eq",), tag="geq")
            for il in range(4):
                sl = slice(il * 8, il * 8 + 8)
                p.op("vector", lambda e, sl=sl, g2=g2, eq=eq, gsb=gsb: e.scalar_tensor_tensor(
                    g2[:, sl], eq[:, sl], -1e30, gsb[:, sl], ALU.mult, ALU.add),
                    reads=("g_eq", "g_gsb"), writes=("g_g2",), tag="gg2")
            g23 = g2.rearrange("p (t e) -> p t e", e=E)
            p.op("vector", lambda e, m2=m2, g23=g23: e.reduce_max(m2, g23, axis=AX.X),
                 reads=("g_g2",), writes=("g_m2",), tag="gc6")
            for il in range(4):
                sl = slice(il * 8, il * 8 + 8)
                p.op("vector", lambda e, sl=sl, il=il, mk2=mk2, gsb=gsb, m2=m2: e.tensor_scalar(
                    mk2[:, sl], gsb[:, sl], m2[:, il:il + 1], None, ALU.is_ge),
                    reads=("g_gsb", "g_m2"), writes=("g_mk",), tag="gmk")
            for il in range(4):
                sl = slice(il * 8, il * 8 + 8)
                p.op("vector", lambda e, sl=sl, il=il, exps=exps, rec=rec: e.tensor_scalar(
                    exps[:, sl], exps[:, sl], rec[:, il:il + 1], None, ALU.mult),
                    reads=("g_rec", "g_exps"), writes=("g_exps",), tag="gsm")
            cmb = p.op("vector", lambda e, gi=gi, exps=exps, mk2=mk2: e.tensor_mul(
                k.combine[:, gi * 32:(gi + 1) * 32], exps, mk2),
                reads=("g_exps", "g_mk"), writes=(f"comb{gi}",), tag="gcomb")
            trs = None
            for il in range(4):
                i = gi * 4 + il
                trs = p.op("tensor", lambda e, i=i, il=il: e.transpose(
                    k.PSB[0:8, il * 128:(il + 1) * 128],
                    k.combine[:, i * E:(i + 1) * E], k.identf[:]),
                    deps=(cmb, self.ld_identf), reads=(f"comb{gi}", "identf"),
                    writes=("psB0",), tag=f"trcmb{i}")
            p.op("vector", lambda e, gi=gi: e.tensor_copy(
                k.combineT[:, gi * 512:(gi + 1) * 512], k.PSB[0:8, 0:512]),
                deps=(trs,), reads=("psB0",), writes=(f"combT{gi}",), tag="evcmb")

        self.dbg_dump(f"combine{b}", k.combine[:], [128, NT * E], F32,
                      tuple(f"comb{gi}" for gi in range(NG)))

    EORD = (5, 4, 2, 3, 6, 7, 0, 1)

    def phase_C(self, b):
        k, p = self, self.p
        for i in range(NT):
            gi = i // 4
            par = i % 2
            mot = k.mo[:, par * D:(par + 1) * D]
            mkey = f"mo{par}"
            cb = None
            for h2 in range(2):
                cb = p.op("tensor", lambda e, i=i, h2=h2: e.matmul(
                    k.PSB[:, h2 * 512:(h2 + 1) * 512],
                    k.combineT[:, i * 128:(i + 1) * 128],
                    k.be[:, h2 * 512:(h2 + 1) * 512],
                    start=True, stop=True, skip_group_check=True),
                    reads=(f"combT{gi}", "be"),
                    writes=("psB0", "psB1") if h2 == 0 else (), tag=f"cb{i}")
            prev = p.op("vector", lambda e, mot=mot: e.tensor_copy(mot, k.PSB[:]),
                        deps=(cb,), reads=("psB0", "psB1"), writes=(mkey,), tag=f"evcb{i}")
            for pp, ei in enumerate(self.EORD):
                ps = (k.PSA, k.PSC)[pp % 2]
                pskeys = ("psA0", "psA1") if pp % 2 == 0 else ("psC0", "psC1")
                mm = None
                for c in range(NCH):
                    for h2 in range(2):
                        mm = p.op("tensor", lambda e, c=c, h2=h2, ei=ei, i=i, ps=ps: e.matmul(
                            ps[:, h2 * 512:(h2 + 1) * 512],
                            k.xnT[:, c * S + i * 128:c * S + (i + 1) * 128],
                            k.weW[ei][:, c * D + h2 * 512:c * D + (h2 + 1) * 512],
                            start=(c == 0), stop=(c == NCH - 1), skip_group_check=True),
                            reads=(f"xnT{i}", f"E{ei}"),
                            writes=pskeys if (c == 0 and h2 == 0) else (), tag=f"ex{i}_{ei}")
                prev = p.op("vector", lambda e, mot=mot, ps=ps, i=i, ei=ei: e.scalar_tensor_tensor(
                    mot, ps[:], k.combine[:, i * E + ei:i * E + ei + 1], mot, ALU.mult, ALU.add),
                    deps=(mm, prev), reads=pskeys + (f"comb{gi}", mkey),
                    writes=(mkey,), tag=f"stt{i}_{ei}")
            p.op("sync", lambda e, mot=mot, i=i: e.dma_start(
                out=k.moe_h[b, i * 128:(i + 1) * 128, :], in_=mot),
                reads=(mkey,), dma_sem="qS", tag=f"stmo{i}")

    def phase_B1a(self, b):
        k, p = self, self.p
        self.wmat_load(k.wk_h, k.wkW, writes=("wkW", "wkWb", "E5"), tag="wk")
        self.wmat_load(k.wv_h, k.wvW, writes=("wvW", "wvWb", "E4"), tag="wv")
        self.wmat_load(k.wm_h, k.wmW, writes=("wmW", "E6", "E7"), tag="wm")
        v_ones = k.vbuf[:].rearrange("p (s c) -> p s c", c=65)[:, :, 64:65]
        p.op("gpsimd", lambda e: e.memset(v_ones, 1.0),
             writes=("vones", "E2", "E3"), tag="vones")
        self.proj_kv(b, k.wkW, ("wkW", "wkWb"), k.wvW, ("wvW", "wvWb"),
                     k.bkT, "bkT", k.brow_v, "brow_v")

    def phase_B2a(self, b):
        k, p = self, self.p

        def slot_ops(names):
            dd = []
            for n in names:
                s = p.slots.get(n)
                if s:
                    if s["w"] is not None:
                        dd.append(s["w"])
                    dd.extend(s["r"])
            return tuple(dd)

        def half_load(src2d, dst, half, deps, writes, tag):
            src = src2d[half * 512:(half + 1) * 512, :].rearrange(
                "(c p) d -> p c d", p=128)
            d3 = dst[:, half * 4 * D:(half + 1) * 4 * D].rearrange(
                "p (c d) -> p c d", c=4)
            return p.op("gpsimd", lambda g: g.dma_start(out=d3, in_=src),
                        deps=deps, writes=writes, dma_sem="qG", tag=tag)

        half_load(k.wqkv_h[:, 1024:2048], k.wkW, 0,
                  slot_ops([f"pT{s}" for s in range(4)]), ("wkW",), "wk2a")
        half_load(k.wqkv_h[:, 1024:2048], k.wkW, 1,
                  slot_ops([f"a1n{il}" for il in range(4)]), ("wkWb",), "wk2b")
        half_load(k.wqkv_h[:, 2048:3072], k.wvW, 0,
                  slot_ops([f"qT{d}" for d in range(NCH)]), ("wvW",), "wv2a")
        half_load(k.wqkv_h[:, 2048:3072], k.wvW, 1,
                  slot_ops([f"a1nT{pq}_{c}" for pq in range(2) for c in range(NCH)]
                           + ["outT1"]), ("wvWb",), "wv2b")
        self.proj_kv(b, k.wkW, ("wkW", "wkWb"), k.wvW, ("wvW", "wvWb"),
                     k.bk2T, "bk2T", k.brow_v2, "brow_v2")

    def proj_kv(self, b, wkW, wk_keys, wvW, wv_keys, bkT, bk_key, brow_v, bv_key):
        k, p = self, self.p

        def kt_unit(u):
            d, tseg = u // NG, u % NG
            par = u % 2
            pskey = f"psA{par}"
            pcol = par * 512
            mm = None
            for c in range(NCH):
                mm = p.op("tensor", lambda e, c=c, d=d, tseg=tseg, pcol=pcol: e.matmul(
                    k.PSA[:, pcol:pcol + 512],
                    wkW[:, c * D + d * 128:c * D + (d + 1) * 128],
                    k.xnT[:, c * S + tseg * 512:c * S + (tseg + 1) * 512],
                    start=(c == 0), stop=(c == NCH - 1), skip_group_check=True),
                    reads=(wk_keys[c // 4],) + tuple(f"xnT{i}" for i in range(tseg * 4, tseg * 4 + 4)),
                    writes=(pskey,) if c == 0 else (), tag=f"kt{d}_{tseg}")
            p.op("vector", lambda e, d=d, tseg=tseg, pcol=pcol: e.tensor_scalar(
                k.kT[:, d * S + tseg * 512:d * S + (tseg + 1) * 512],
                k.PSA[:, pcol:pcol + 512], bkT[:, d:d + 1], None, ALU.add),
                deps=(mm,), reads=(pskey, bk_key), writes=("kT",), tag=f"evkt{d}_{tseg}")

        def v_unit(i):
            ps = (k.PSB, k.PSC)[i % 2]
            pskeys = ("psB0", "psB1") if i % 2 == 0 else ("psC0", "psC1")
            for c in range(NCH):
                for h2 in range(2):
                    p.op("tensor", lambda e, c=c, h2=h2, i=i, ps=ps: e.matmul(
                        ps[:, h2 * 512:(h2 + 1) * 512],
                        k.xnT[:, c * S + i * 128:c * S + (i + 1) * 128],
                        wvW[:, c * D + h2 * 512:c * D + (h2 + 1) * 512],
                        start=(c == 0), stop=False, skip_group_check=True),
                        reads=(wv_keys[c // 4], f"xnT{i}"),
                        writes=pskeys if (c == 0 and h2 == 0) else (), tag=f"v{i}")
            mm = None
            for h2 in range(2):
                mm = p.op("tensor", lambda e, h2=h2, ps=ps: e.matmul(
                    ps[:, h2 * 512:(h2 + 1) * 512], k.onesb[:],
                    brow_v[:, h2 * 512:(h2 + 1) * 512],
                    start=False, stop=True, skip_group_check=True),
                    reads=("onesb", bv_key), tag=f"vb{i}")
            dst = k.vbuf[:].rearrange("p (s c) -> p s c", c=65)[:, i * H:(i + 1) * H, 0:64]
            src = ps[:].rearrange("p (s c) -> p s c", c=64)
            p.op("vector", lambda e, dst=dst, src=src: e.tensor_copy(dst, src),
                 deps=(mm,), reads=pskeys, writes=("vbuf",), tag=f"evv{i}")

        for i in range(NT):
            kt_unit(2 * i)
            v_unit(i)
            kt_unit(2 * i + 1)

    def attn(self, b, causal):
        k, p = self, self.p
        wq_src = k.wq_h if causal else k.wqkv_h[:, 0:1024]
        bqT = k.bqT if causal else k.bq2T
        bq_key = "bqT" if causal else "bq2T"

        def emit_epilogue(g):
            for il in range(4):
                i = g * 4 + il
                par = il % 2
                if causal:
                    xop = p.op("sync", lambda e, i=i, par=par: e.dma_start(
                        out=k.xt[:, par * D:(par + 1) * D],
                        in_=k.x_h[b, i * 128:(i + 1) * 128, :]),
                        writes=(f"xt{par}",), dma_sem="qS", tag=f"rlx{i}")
                    for c in range(NCH):
                        pse = (k.PSE1, k.PSE2)[c % 2]
                        psekey = f"psE{c % 2 + 1}"
                        tt = p.op("tensor", lambda e, c=c, il=il, pse=pse: e.transpose(
                            pse[:, 512:640], k.a1n[:, il * D + c * 128:il * D + (c + 1) * 128],
                            k.identb[:]),
                            reads=(f"a1n{il}",), writes=(psekey,), tag=f"tb{i}_{c}")
                        p.op("vector", lambda e, c=c, par=par, pse=pse: e.tensor_copy(
                            k.a1nT[:, par * D + c * 128:par * D + (c + 1) * 128], pse[:, 512:640]),
                            deps=(tt,), reads=(psekey,), writes=(f"a1nT{par}_{c}",), tag=f"evtb{i}_{c}")
                    mm = None
                    for c in range(NCH):
                        for h2 in range(2):
                            mm = p.op("tensor", lambda e, c=c, h2=h2, par=par: e.matmul(
                                k.PSB[:, h2 * 512:(h2 + 1) * 512],
                                k.a1nT[:, par * D + c * 128:par * D + (c + 1) * 128],
                                k.wmW[:, c * D + h2 * 512:c * D + (h2 + 1) * 512],
                                start=(c == 0), stop=False, skip_group_check=True),
                                reads=(f"a1nT{par}_{c}", "wmW"),
                                writes=("psB0", "psB1") if (c == 0 and h2 == 0) else (),
                                tag=f"wm{i}")
                    for h2 in range(2):
                        mm = p.op("tensor", lambda e, h2=h2: e.matmul(
                            k.PSB[:, h2 * 512:(h2 + 1) * 512], k.onesb[:],
                            k.brow_m[:, h2 * 512:(h2 + 1) * 512],
                            start=False, stop=True, skip_group_check=True),
                            reads=("onesb", "brow_m"), tag=f"wmb{i}")
                    p.op("vector", lambda e, par=par: e.tensor_add(
                        k.acc[:, par * D:(par + 1) * D], k.PSB[:],
                        k.xt[:, par * D:(par + 1) * D]),
                        deps=(mm, xop), reads=("psB0", "psB1", f"xt{par}"),
                        writes=(f"acc{par}",), tag=f"pacc{i}")
                    p.op("sync", lambda e, i=i, par=par: e.dma_start(
                        out=k.partial_h[b, i * 128:(i + 1) * 128, :],
                        in_=k.acc[:, par * D:(par + 1) * D]),
                        reads=(f"acc{par}",), dma_sem="qS", tag=f"stp{i}")
                else:
                    pop = p.op("sync", lambda e, i=i, par=par: e.dma_start(
                        out=k.xt[:, par * D:(par + 1) * D],
                        in_=k.partial_h[b, i * 128:(i + 1) * 128, :]),
                        writes=(f"xt{par}",), dma_sem="qS", tag=f"ldp{i}")
                    mop = p.op("sync", lambda e, i=i, par=par: e.dma_start(
                        out=k.mo[:, par * D:(par + 1) * D],
                        in_=k.moe_h[b, i * 128:(i + 1) * 128, :]),
                        writes=(f"mo{par}",), dma_sem="qS", tag=f"ldmo{i}")
                    f1 = p.op("vector", lambda e, par=par, il=il: e.tensor_add(
                        k.acc[:, par * D:(par + 1) * D],
                        k.xt[:, par * D:(par + 1) * D],
                        k.a1n[:, il * D:(il + 1) * D]),
                        deps=(pop,), reads=(f"xt{par}", f"a1n{il}"),
                        writes=(f"acc{par}",), tag=f"fin1_{i}")
                    p.op("vector", lambda e, par=par: e.tensor_add(
                        k.acc[:, par * D:(par + 1) * D],
                        k.acc[:, par * D:(par + 1) * D],
                        k.mo[:, par * D:(par + 1) * D]),
                        deps=(mop, f1), reads=(f"mo{par}", f"acc{par}"),
                        writes=(f"acc{par}",), tag=f"fin2_{i}")
                    self.yst = p.op("sync", lambda e, i=i, par=par: e.dma_start(
                        out=k.y_h[b, i * 128:(i + 1) * 128, :],
                        in_=k.acc[:, par * D:(par + 1) * D]),
                        reads=(f"acc{par}",), dma_sem="qS", tag=f"sty{i}")

        pend_epi = None
        for g in range(NG):
            cmax = 4 * g + 3 if causal else NT - 1
            for d in range(NCH):
                par = d % 3
                ppar = d % 2
                pcol = ppar * 512
                pskey = f"psA{ppar}"
                wsl = k.wqW[:, par * 1024:(par + 1) * 1024]
                wkey = f"wq{par}"
                src = wq_src.rearrange("(c p) d -> p c d", p=128)[:, :, d * 128:(d + 1) * 128]
                d3 = wsl.rearrange("p (c t) -> p c t", c=NCH)
                wop = p.op("gpsimd", lambda e, d3=d3, src=src: e.dma_start(out=d3, in_=src),
                           writes=(wkey,), dma_sem="qG", tag=f"wqd{g}_{d}")
                mm = None
                for c in range(NCH):
                    mm = p.op("tensor", lambda e, c=c, g=g, wsl=wsl, pcol=pcol: e.matmul(
                        k.PSA[:, pcol:pcol + 512],
                        wsl[:, c * 128:(c + 1) * 128],
                        k.xnT[:, c * S + g * QG:c * S + (g + 1) * QG],
                        start=(c == 0), stop=(c == NCH - 1), skip_group_check=True),
                        deps=(wop,),
                        reads=(wkey,) + tuple(f"xnT{i}" for i in range(g * 4, g * 4 + 4)),
                        writes=(pskey,) if c == 0 else (), tag=f"qt{g}_{d}")
                p.op("vector", lambda e, d=d, g=g, bqT=bqT, pcol=pcol: e.tensor_scalar(
                    k.qTb[:, d * QG:(d + 1) * QG], k.PSA[:, pcol:pcol + 512],
                    bqT[:, d:d + 1], None, ALU.add),
                    deps=(mm,), reads=(pskey, bq_key), writes=(f"qT{d}",), tag=f"evqt{g}_{d}")
            if pend_epi is not None:
                emit_epilogue(pend_epi)
                pend_epi = None
            def emit_finish_il(pr, ob, obkey, il):
                hA, hB = 2 * pr, 2 * pr + 1
                if True:
                    par = (pr * 4 + il) % 2
                    pse = (k.PSE1, k.PSE2)[par]
                    psekey = f"psE{par + 1}"
                    p.op("tensor", lambda e, il=il, pse=pse, ob=ob: e.transpose(
                        pse[:, 0:65], ob[0:65, il * 128:(il + 1) * 128], k.identb[0:65, 0:65]),
                        reads=(obkey,), writes=(psekey,), tag=f"t1{g}_{pr}_{il}")
                    t2 = p.op("tensor", lambda e, il=il, pse=pse, ob=ob: e.transpose(
                        pse[:, 128:193], ob[0:65, 512 + il * 128:512 + (il + 1) * 128],
                        k.identb[0:65, 0:65]),
                        reads=(obkey,), writes=(psekey,), tag=f"t2{g}_{pr}_{il}")
                    recA = k.small[:, 8 + par * 2:9 + par * 2]
                    recB = k.small[:, 9 + par * 2:10 + par * 2]
                    r1 = p.op("vector", lambda e, pse=pse, recA=recA: e.reciprocal(recA, pse[:, 64:65]),
                              deps=(t2,), reads=(psekey,), writes=(f"recA{par}",), tag="rcA")
                    r2 = p.op("vector", lambda e, pse=pse, recB=recB: e.reciprocal(recB, pse[:, 192:193]),
                              reads=(psekey,), writes=(f"recB{par}",), tag="rcB")
                    dstA = k.a1n[:, il * D + hA * 64: il * D + hA * 64 + 64]
                    dstB = k.a1n[:, il * D + hB * 64: il * D + hB * 64 + 64]
                    p.op("vector", lambda e, pse=pse, dstA=dstA, recA=recA: e.tensor_scalar(
                        dstA, pse[:, 0:64], recA[:, 0:1], None, ALU.mult),
                        deps=(r1,), reads=(psekey, f"recA{par}"),
                        writes=(f"a1n{il}",), tag="nrmA")
                    p.op("vector", lambda e, pse=pse, dstB=dstB, recB=recB: e.tensor_scalar(
                        dstB, pse[:, 128:192], recB[:, 0:1], None, ALU.mult),
                        deps=(r2,), reads=(psekey, f"recB{par}"),
                        writes=(f"a1n{il}",), tag="nrmB")
            def finish_steps(pr, ob, obkey):
                for il in range(4):
                    emit_finish_il(pr, ob, obkey, il)
                    yield

            def pair_steps(pr):
                hA, hB = 2 * pr, 2 * pr + 1
                av_last = None
                pend = []
                def emit_av(dep_probs, c, slot):
                    nonlocal av_last
                    p.op("tensor", lambda e, c=c, hA=hA, slot=slot, cmax=cmax: e.matmul(
                        k.PSC[0:65, 0:512],
                        k.vbuf[:, (c * H + hA) * 65:(c * H + hA) * 65 + 65],
                        k.probsT[:, slot * 1024:slot * 1024 + 512],
                        start=(c == 0), stop=(c == cmax), skip_group_check=True),
                        deps=(dep_probs,), reads=("vbuf", "vones", f"pT{slot}"),
                        writes=("psC0",) if c == 0 else (), tag=f"avA{g}_{pr}_{c}")
                    av_last = p.op("tensor", lambda e, c=c, hB=hB, slot=slot, cmax=cmax: e.matmul(
                        k.PSC[0:65, 512:1024],
                        k.vbuf[:, (c * H + hB) * 65:(c * H + hB) * 65 + 65],
                        k.probsT[:, slot * 1024 + 512:slot * 1024 + 1024],
                        start=(c == 0), stop=(c == cmax), skip_group_check=True),
                        deps=(dep_probs,), reads=("vbuf", "vones", f"pT{slot}"),
                        writes=("psC1",) if c == 0 else (), tag=f"avB{g}_{pr}_{c}")
                for c in range(cmax + 1):
                    yield
                    sps = (k.PSA, k.PSB)[c % 2]
                    skeys = ("psA0", "psA1") if c % 2 == 0 else ("psB0", "psB1")
                    slot = c % 4
                    p.op("tensor", lambda e, c=c, pr=pr, sps=sps: e.matmul(
                        sps[:, 0:512],
                        k.kT[0:64, pr * S + c * 128:pr * S + (c + 1) * 128],
                        k.qTb[0:64, pr * QG:(pr + 1) * QG],
                        start=True, stop=True, skip_group_check=True,
                        tile_position=(0, 0)),
                        reads=("kT", f"qT{pr}"), writes=(skeys[0],), tag=f"scA{g}_{pr}_{c}")
                    sB = p.op("tensor", lambda e, c=c, pr=pr, sps=sps: e.matmul(
                        sps[:, 512:1024],
                        k.kT[64:128, pr * S + c * 128:pr * S + (c + 1) * 128],
                        k.qTb[64:128, pr * QG:(pr + 1) * QG],
                        start=True, stop=True, skip_group_check=True,
                        tile_position=(64, 0)),
                        reads=("kT", f"qT{pr}"), writes=(skeys[1],), tag=f"scB{g}_{pr}_{c}")
                    ex = p.op("scalar", lambda e, slot=slot, sps=sps: e.activation(
                        k.probsT[:, slot * 1024:(slot + 1) * 1024], sps[:],
                        AF.Exp, bias=-CSH, scale=0.125),
                        deps=(sB,), reads=skeys + ("c_csh",), writes=(f"pT{slot}",), tag=f"exp{g}_{pr}_{c}")
                    dep_probs = ex
                    if causal and c >= 4 * g:
                        j = c - 4 * g
                        msl = k.masks[:, j * 512:(j + 1) * 512]
                        for hh in range(2):
                            sl = k.probsT[:, slot * 1024 + hh * 512: slot * 1024 + hh * 512 + 512]
                            dep_probs = p.op("vector", lambda e, sl=sl, msl=msl: e.tensor_mul(
                                sl, sl, msl),
                                deps=(dep_probs,), reads=(f"pT{slot}", "masks"),
                                writes=(f"pT{slot}",), tag=f"msk{g}_{pr}_{c}")
                    pend.append((dep_probs, c, slot))
                    if c % 2 == 1:
                        while len(pend) > 2:
                            emit_av(*pend.pop(0))
                while pend:
                    emit_av(*pend.pop(0))
                ob = (k.outT, k.outT2)[pr % 2]
                obkey = f"outT{pr % 2}"
                p.op("vector", lambda e, ob=ob: e.tensor_copy(ob[0:65, :], k.PSC[0:65, :]),
                     deps=(av_last,), reads=("psC0", "psC1"),
                     writes=(obkey,), tag=f"evout{g}_{pr}")

            fin_gen = None
            for pr in range(NPAIR):
                pg = pair_steps(pr)
                si = 0
                for _ in pg:
                    if fin_gen is not None and si % 2 == 1:
                        next(fin_gen, None)
                        next(fin_gen, None)
                    si += 1
                while fin_gen is not None and next(fin_gen, StopIteration) is not StopIteration:
                    pass
                fin_gen = finish_steps(pr, (k.outT, k.outT2)[pr % 2], f"outT{pr % 2}")
            while next(fin_gen, StopIteration) is not StopIteration:
                pass
            pend_epi = g
        emit_epilogue(pend_epi)

    def phase_fin(self):
        p = self.p
        sync_lasts = [o for s, o in p.lane_last.items() if s.startswith("qS")]
        gp_lasts = [o for s, o in p.lane_last.items() if s.startswith("qG")]
        p.op("sync", lambda e: e.nop(), deps=[self.yst] + sync_lasts, tag="finS")
        p.op("gpsimd", lambda e: e.nop(), deps=gp_lasts, tag="finG")


def build_program(dbg=()):
    kb = KB(dbg)
    kb.build()
    return kb


def _make_masks():
    k = np.arange(128)[:, None]
    q = np.arange(512)[None, :]
    cols = [((q - k) >= 128 * j).astype(np.float32) for j in range(4)]
    return np.concatenate(cols, axis=1).astype(ml_dtypes.bfloat16)


def make_in_maps(inputs):
    x = np.asarray(inputs["L0_attn_x"])
    common = {
        "wq": np.asarray(inputs["L0_attn_Wq"]), "bq": np.asarray(inputs["L0_attn_bq"]),
        "wk": np.asarray(inputs["L0_attn_Wk"]), "bk": np.asarray(inputs["L0_attn_bk"]),
        "wv": np.asarray(inputs["L0_attn_Wv"]), "bv": np.asarray(inputs["L0_attn_bv"]),
        "wm": np.asarray(inputs["L0_attn_Wm"]), "bm": np.asarray(inputs["L0_attn_bm"]),
        "wqkv": np.asarray(inputs["L0_ffn_W_qkv"]), "bqkv": np.asarray(inputs["L0_ffn_b_qkv"]),
        "wg": np.asarray(inputs["L0_ffn_W_gate"]), "bg": np.asarray(inputs["L0_ffn_b_gate"]),
        "we": np.asarray(inputs["L0_ffn_W_experts"]), "be": np.asarray(inputs["L0_ffn_b_experts"]),
        "identb": np.eye(128, dtype=np.float32).astype(ml_dtypes.bfloat16),
        "identf": np.eye(128, dtype=np.float32),
        "onesb": np.ones((1, 128), dtype=np.float32).astype(ml_dtypes.bfloat16),
        "masks": _make_masks(),
    }
    in_maps = []
    for core in range(8):
        m = dict(common)
        m["x"] = np.ascontiguousarray(x[core * BPC:(core + 1) * BPC])
        in_maps.append(m)
    return in_maps


def kernel(**inputs):
    kb = build_program()
    in_maps = make_in_maps(inputs)
    res = run_bass_kernel_spmd(kb.nc, in_maps, core_ids=list(range(8)))
    return np.concatenate([r["y"] for r in res.results], axis=0)
